# version 04
# baseline (speedup 1.0000x reference)
import math
import numpy as np
from contextlib import ExitStack

import concourse.bass as bass
import concourse.mybir as mybir
from concourse.bass_utils import run_bass_kernel_spmd
from concourse.aot_env import allow_ici_all_to_all

F32 = mybir.dt.float32
BF16 = mybir.dt.bfloat16
AF = mybir.ActivationFunctionType
ALU = mybir.AluOpType

NCORES = 8
NH, HKV, G, HD = 32, 8, 4, 128
DM, DH, RANK = 4096, 11008, 16
NDC = DM // 128
NHC = DH // 128
NHH = DH // 2
EPS = 1e-5
ISQ = 1.0 / math.sqrt(HD)
ENGS = ("pe", "act", "dve", "sync", "gp")
SEC = {"pe": "tensor", "act": "scalar", "dve": "vector", "sync": "sync", "gp": "gpsimd"}

SEM_NAMES = ["s_gw", "s_cc", "s_x", "s_of", "s_lga", "s_lgb", "s_qa", "s_qb",
             "s_woa", "s_wob", "s_gua", "s_gub", "s_dwa", "s_dwb",
             "s_xta", "s_xtb", "s_x1ta", "s_x1tb", "s_st", "s_gst",
             "s_qe0", "s_qe1", "s_ob0", "s_ob1", "s_ot0", "s_ot1",
             "s_1sa", "s_1sb", "s_dsa", "s_dsb", "s_dla", "s_dlb",
             "s_pe", "s_act", "s_dve"]


def build(S):
    CH = S // 16
    TPC = S // 8
    KPC = CH // 128
    NKT = S // 128
    assert CH % 128 == 0 and 128 <= CH <= 512

    nc = bass.Bass("TRN2", target_bir_lowering=False, debug=False)

    def din(n, sh):
        return nc.dram_tensor(n, sh, F32, kind="ExternalInput").ap()

    x_fm = din("x_fm", [DM, TPC])
    wq_t = din("wq_t", [4, DM, 128])
    wk_t = din("wk_t", [DM, 128])
    wv_t = din("wv_t", [DM, 128])
    wo_t = din("wo_t", [NDC, DM, 128])
    gw_t = din("gw_t", [NHC, DM + 128, 128])
    uw_t = din("uw_t", [NHC, DM + 128, 128])
    dw_t = din("dw_t", [NDC, DH + 128, 128])
    ga_t = din("ga_t", [DM, RANK])
    ua_t = din("ua_t", [DM, RANK])
    da_t = din("da_t", [DH, RANK])
    masks_i = din("masks_i", [128, KPC, CH])
    ones_i = din("ones_i", [128, 1])
    onesr_i = din("onesr_i", [1, 128])

    out_fm = nc.dram_tensor("out_fm", [DM, TPC], F32, kind="ExternalOutput").ap()

    ln_shard = nc.dram_tensor("ln_shard", [DM, TPC], BF16)
    ln_gat = nc.dram_tensor("ln_gat", [NCORES * DM, TPC], BF16)
    q_spill = nc.dram_tensor("q_spill", [512, S], BF16).ap()
    a2a_in = nc.dram_tensor("a2a_in", [NCORES * 512, TPC], BF16)
    a2a_out = nc.dram_tensor("a2a_out", [NCORES * 512, TPC], BF16)
    x1_d = nc.dram_tensor("x1_d", [DM, TPC], F32).ap()
    dn_d = nc.dram_tensor("dn_d", [DM, TPC], BF16)

    ops = {e: [] for e in ENGS}
    cnt = {}
    SEM = {}

    def inc(sem, k=1):
        cnt[sem] = cnt.get(sem, 0) + k
        return cnt[sem]

    def cur(sem):
        return cnt.get(sem, 0)

    def w(eng, sem, val):
        if val and val > 0:
            ops[eng].append(lambda E, s=sem, v=val: E.wait_ge(SEM[s], v))

    def gp_load(dst, src, sem):
        v = inc(sem, 16)
        ops["gp"].append(lambda E, o=dst, i=src, s=sem: E.dma_start(out=o, in_=i).then_inc(SEM[s], 16))
        return v

    def sy_load(dst, src, sem):
        v = inc(sem, 16)
        ops["sync"].append(lambda E, o=dst, i=src, s=sem: E.dma_start(out=o, in_=i).then_inc(SEM[s], 16))
        return v

    def sy_store(dst, src, sem="s_st"):
        v = inc(sem, 16)
        ops["sync"].append(lambda E, o=dst, i=src, s=sem: E.dma_start(out=o, in_=i).then_inc(SEM[s], 16))
        return v

    def gp_store(dst, src, sem="s_gst"):
        v = inc(sem, 16)
        ops["gp"].append(lambda E, o=dst, i=src, s=sem: E.dma_start(out=o, in_=i).then_inc(SEM[s], 16))
        return v

    def pe(fn, tick=True):
        if tick:
            v = inc("s_pe")
            ops["pe"].append(lambda E, f=fn: f(E).then_inc(SEM["s_pe"], 1))
            return v
        ops["pe"].append(lambda E, f=fn: f(E))
        return None

    def act(fn):
        v = inc("s_act")
        ops["act"].append(lambda E, f=fn: f(E).then_inc(SEM["s_act"], 1))
        return v

    def dve(fn):
        v = inc("s_dve")
        ops["dve"].append(lambda E, f=fn: f(E).then_inc(SEM["s_dve"], 1))
        return v

    STORE_SEMS = ("s_gst", "s_st", "s_qe0", "s_qe1", "s_ob0", "s_ob1", "s_ot0", "s_ot1", "s_1sa", "s_1sb", "s_dsa", "s_dsb")

    def fence(engines=ENGS, sems=("s_pe", "s_act", "s_dve")):
        marks = {k: cur(k) for k in sems + STORE_SEMS}
        for e in engines:
            for sname, v in marks.items():
                if v:
                    w(e, sname, v)

    st = ExitStack()

    ones_b = st.enter_context(nc.sbuf_tensor("ones_b", [128, 1], BF16))
    onesr_f = st.enter_context(nc.sbuf_tensor("onesr_f", [1, 128], F32))
    onesr_b = st.enter_context(nc.sbuf_tensor("onesr_b", [1, 128], BF16))
    ga_sb = st.enter_context(nc.sbuf_tensor("ga_sb", [128, NDC, RANK], BF16))
    ua_sb = st.enter_context(nc.sbuf_tensor("ua_sb", [128, NDC, RANK], BF16))
    da_sb = st.enter_context(nc.sbuf_tensor("da_sb", [128, NHC, RANK], BF16))
    r_sum = st.enter_context(nc.sbuf_tensor("r_sum", [1, CH], F32))
    r_sq = st.enter_context(nc.sbuf_tensor("r_sq", [1, CH], F32))
    r_mean = st.enter_context(nc.sbuf_tensor("r_mean", [1, CH], F32))
    r_var = st.enter_context(nc.sbuf_tensor("r_var", [1, CH], F32))
    r_tmp = st.enter_context(nc.sbuf_tensor("r_tmp", [1, CH], F32))
    r_rstd = st.enter_context(nc.sbuf_tensor("r_rstd", [1, CH], F32))
    r_srow = [st.enter_context(nc.sbuf_tensor(f"r_srow{i}", [1, CH], F32)) for i in range(2)]
    r_rinv = [st.enter_context(nc.sbuf_tensor(f"r_rinv{i}", [1, CH], F32)) for i in range(2)]
    r_rinvb = [st.enter_context(nc.sbuf_tensor(f"r_rinvb{i}", [1, CH], BF16)) for i in range(2)]

    stA = st.enter_context(nc.psum_tensor("stA", [128, 2 * CH], F32))
    stB = st.enter_context(nc.psum_tensor("stB", [128, 2 * CH], F32))
    pA = st.enter_context(nc.psum_tensor("pA", [128, CH], F32))
    pB = st.enter_context(nc.psum_tensor("pB", [128, CH], F32))
    pC = st.enter_context(nc.psum_tensor("pC", [128, CH], F32))
    pD = st.enter_context(nc.psum_tensor("pD", [128, CH], F32))

    for n in SEM_NAMES:
        SEM[n] = st.enter_context(nc.semaphore(n))
    block = st.enter_context(nc.Block())

    def flush():
        for e in ENGS:
            if ops[e]:
                fns = ops[e]
                ops[e] = []

                def run(E, fns=fns):
                    for f in fns:
                        f(E)
                getattr(block, SEC[e])(run)

    mean_bc = bass.AP(stA, 0, [[2 * CH, 128], [0, NDC], [1, CH]])
    rstd_bc = bass.AP(stA, CH, [[2 * CH, 128], [0, NDC], [1, CH]])
    mean_t = bass.AP(stA, 0, [[2 * CH, 128], [1, CH]])
    rstd_t = bass.AP(stA, CH, [[2 * CH, 128], [1, CH]])

    def ln_rows(sum_bank, sq_bank, pe_gate, stA_gate=0):
        w("act", "s_pe", pe_gate)
        v1 = act(lambda E: E.copy(r_sum[:, :], sum_bank[0:1, 0:CH]))
        v2 = act(lambda E: E.copy(r_sq[:, :], sq_bank[0:1, 0:CH]))
        w("dve", "s_act", v2)
        a1 = dve(lambda E: E.tensor_scalar_mul(r_mean[:, :], r_sum[:, :], 1.0 / DM))
        w("dve", "s_dve", a1)
        a2 = dve(lambda E: E.tensor_scalar_mul(r_sq[:, :], r_sq[:, :], 1.0 / DM))
        w("dve", "s_dve", a2)
        a3 = dve(lambda E: E.tensor_mul(r_tmp[:, :], r_mean[:, :], r_mean[:, :]))
        w("dve", "s_dve", a3)
        a4 = dve(lambda E: E.tensor_sub(r_var[:, :], r_sq[:, :], r_tmp[:, :]))
        w("dve", "s_dve", a4)
        a4b = dve(lambda E: E.tensor_scalar_add(r_var[:, :], r_var[:, :], EPS))
        w("act", "s_dve", a4b)
        a5 = act(lambda E: E.activation(r_tmp[:, :], r_var[:, :], AF.Ln))
        w("act", "s_act", a5)
        a6 = act(lambda E: E.activation(r_rstd[:, :], r_tmp[:, :], AF.Exp, scale=-0.5))
        w("pe", "s_act", a6)
        if stA_gate:
            w("pe", "s_dve", stA_gate)
        pe(lambda E: E.matmul(stA[:, 0:CH], onesr_f[:, :], r_mean[:, :], start=True, stop=True), tick=False)
        b2 = pe(lambda E: E.matmul(stA[:, CH:2 * CH], onesr_f[:, :], r_rstd[:, :], start=True, stop=True))
        return b2, v2

    gp_load(ga_sb[:, :, :], ga_t.rearrange("(c p) r -> p c r", p=128), "s_gw")
    gp_load(ua_sb[:, :, :], ua_t.rearrange("(c p) r -> p c r", p=128), "s_gw")
    gp_load(da_sb[:, :, :], da_t.rearrange("(c p) r -> p c r", p=128), "s_gw")
    gp_load(ones_b[:, :], ones_i, "s_gw")
    v_ob = gp_load(onesr_b[:, :], onesr_i, "s_gw")
    v_of = sy_load(onesr_f[:, :], onesr_i, "s_of")

    p1 = ExitStack()
    x_sb = p1.enter_context(nc.sbuf_tensor("x_sb", [128, NDC, CH], F32))
    ln1b = p1.enter_context(nc.sbuf_tensor("ln1b", [128, NDC, CH], BF16))
    xb_t = p1.enter_context(nc.sbuf_tensor("xb_t", [128, CH], BF16))
    sq_t = p1.enter_context(nc.sbuf_tensor("sq_t", [128, CH], BF16))

    x_r = x_fm.rearrange("(c p) t -> p c t", p=128)
    NCHK = TPC // CH
    vb = 0
    stat_free = 0
    for cc in range(NCHK):
        vx = sy_load(x_sb[:, :, :], x_r[:, :, cc * CH:(cc + 1) * CH], "s_x")
        w("dve", "s_x", vx)
        m2 = 0
        for dc in range(NDC):
            if dc > 0:
                w("dve", "s_pe", m2)
            vc = dve(lambda E, dc=dc: E.tensor_copy(xb_t[:, :], x_sb[:, dc, :]))
            w("dve", "s_dve", vc)
            vs = dve(lambda E: E.tensor_mul(sq_t[:, :], xb_t[:, :], xb_t[:, :]))
            w("pe", "s_dve", vs)
            if cc == 0 and dc == 0:
                w("pe", "s_gw", v_ob)
                w("pe", "s_of", v_of)
            if dc == 0 and stat_free:
                w("pe", "s_act", stat_free)
            pe(lambda E, dc=dc: E.matmul(pC[0:1, :], ones_b[:, :], xb_t[:, :],
                                         start=(dc == 0), stop=(dc == NDC - 1)), tick=False)
            m2 = pe(lambda E, dc=dc: E.matmul(pD[0:1, :], ones_b[:, :], sq_t[:, :],
                                              start=(dc == 0), stop=(dc == NDC - 1)))
        bdone, stat_free = ln_rows(pC, pD, m2, stA_gate=vb)
        w("dve", "s_pe", bdone)
        if cc > 0:
            w("dve", "s_gst", cur("s_gst"))
        lsl = ln1b[:, :, :]
        va = dve(lambda E, l=lsl: E.scalar_tensor_tensor(
            l, x_sb[:, :, :], 1.0, mean_bc, ALU.mult, ALU.subtract))
        w("dve", "s_dve", va)
        vb = dve(lambda E, l=lsl: E.tensor_tensor(l, l, rstd_bc, ALU.mult))
        w("gp", "s_dve", vb)
        gp_store(ln_shard.ap().rearrange("(c p) t -> p c t", p=128)[:, :, cc * CH:(cc + 1) * CH],
                 ln1b[:, :, :])
        if cc + 1 < NCHK:
            w("sync", "s_dve", vb)

    w("gp", "s_gst", cur("s_gst"))
    inc("s_cc")
    ops["gp"].append(lambda E: E.collective_compute(
        "AllGather", ALU.bypass, replica_groups=[list(range(NCORES))],
        ins=[ln_shard.ap()], outs=[ln_gat.ap()]).then_inc(SEM["s_cc"]))

    fence(("pe", "act", "dve"))
    flush()
    p1.close()

    pkv = ExitStack()
    masks_b = pkv.enter_context(nc.sbuf_tensor("masks_b", [128, KPC, CH], BF16))
    k_sb = pkv.enter_context(nc.sbuf_tensor("k_sb", [128, S], BF16))
    v_sb = pkv.enter_context(nc.sbuf_tensor("v_sb", [128, NKT, 128], BF16))

    p1c = ExitStack()
    wq_sb = p1c.enter_context(nc.sbuf_tensor("wq_sb", [128, NDC, 4, 128], BF16))
    wk_sb = p1c.enter_context(nc.sbuf_tensor("wk_sb", [128, NDC, 128], BF16))
    wv_sb = p1c.enter_context(nc.sbuf_tensor("wv_sb", [128, NDC, 128], BF16))
    lg_sb = [p1c.enter_context(nc.sbuf_tensor(f"lg_sb{i}", [128, NDC, CH], BF16)) for i in range(2)]
    q_ev = [p1c.enter_context(nc.sbuf_tensor(f"q_ev{i}", [128, CH], BF16)) for i in range(2)]

    gp_load(masks_b[:, :, :], masks_i, "s_gw")
    for _hh in range(4):
        gp_load(wq_sb[:, :, _hh, :], wq_t[_hh].rearrange("(c p) n -> p c n", p=128), "s_gw")
    gp_load(wk_sb[:, :, :], wk_t.rearrange("(c p) n -> p c n", p=128), "s_gw")
    vwv = gp_load(wv_sb[:, :, :], wv_t.rearrange("(c p) n -> p c n", p=128), "s_gw")
    w("pe", "s_gw", vwv)
    w("sync", "s_cc", 1)

    def lg_src(j):
        r, off = (j, 0) if j < 8 else (15 - j, 1)
        return ln_gat.ap()[DM * r: DM * (r + 1), off * CH:(off + 1) * CH] \
            .rearrange("(c p) t -> p c t", p=128)

    qbanks = [pA, pB, pC, pD]
    lg_gate = [0, 0]
    qev_gate = [0, 0]
    for j in range(16):
        slot = j % 2
        sname = "s_lga" if slot == 0 else "s_lgb"
        w("sync", "s_pe", lg_gate[slot])
        vl = sy_load(lg_sb[slot][:, :, :], lg_src(j), sname)
        w("pe", sname, vl)
        for hh in range(4):
            for dc in range(NDC):
                m = pe(lambda E, hh=hh, dc=dc, s=slot: E.matmul(
                    qbanks[hh][:, :], wq_sb[:, dc, hh, :], lg_sb[s][:, dc, :],
                    start=(dc == 0), stop=(dc == NDC - 1)), tick=(dc == NDC - 1))
            w("act", "s_pe", m)
            w("act", "s_qe0" if hh % 2 == 0 else "s_qe1", qev_gate[hh % 2])
            ve = act(lambda E, hh=hh: E.copy(q_ev[hh % 2][:, :], qbanks[hh][:, :]))
            w("pe", "s_act", ve)
            w("gp", "s_act", ve)
            qev_gate[hh % 2] = gp_store(
                q_spill.rearrange("(h p) t -> p h t", p=128)[:, hh, j * CH:(j + 1) * CH],
                q_ev[hh % 2][:, :], "s_qe0" if hh % 2 == 0 else "s_qe1")
        for dc in range(NDC):
            m = pe(lambda E, dc=dc, s=slot: E.matmul(
                stA[:, 0:CH], wk_sb[:, dc, :], lg_sb[s][:, dc, :],
                start=(dc == 0), stop=(dc == NDC - 1)), tick=(dc == NDC - 1))
        w("act", "s_pe", m)
        ve = act(lambda E, j=j: E.copy(k_sb[:, j * CH:(j + 1) * CH], stA[:, 0:CH]))
        w("pe", "s_act", ve)
        for tt in range(KPC):
            tgt = stA[:, CH:CH + 128] if tt % 2 == 0 else stB[:, 0:128]
            for dc in range(NDC):
                m = pe(lambda E, tt=tt, dc=dc, s=slot, t=tgt: E.matmul(
                    t, lg_sb[s][:, dc, tt * 128:(tt + 1) * 128], wv_sb[:, dc, :],
                    start=(dc == 0), stop=(dc == NDC - 1)), tick=(dc == NDC - 1))
            w("act", "s_pe", m)
            ve = act(lambda E, j=j, tt=tt, t=tgt: E.copy(v_sb[:, j * KPC + tt, :], t))
            w("pe", "s_act", ve)
        lg_gate[slot] = cur("s_pe")

    q_totals = (cur("s_qe0"), cur("s_qe1"))
    fence()
    flush()
    p1c.close()

    p2 = ExitStack()
    q_t = [p2.enter_context(nc.sbuf_tensor(f"q_t{i}", [128, 2 * CH], BF16)) for i in range(2)]
    p_sb = [p2.enter_context(nc.sbuf_tensor(f"p_sb{i}", [128, 2 * CH], BF16)) for i in range(2)]
    o_t = [p2.enter_context(nc.sbuf_tensor(f"o_t{i}", [128, CH], F32)) for i in range(2)]
    o_b = [p2.enter_context(nc.sbuf_tensor(f"o_b{i}", [128, CH], BF16)) for i in range(2)]

    w("sync", "s_qe0", q_totals[0])
    w("sync", "s_qe1", q_totals[1])

    stX = [stA, stB]
    atst = {"o_ready": [0, 0], "sum_free": [0, 0], "ot_free": [0, 0],
            "ob_free": [0, 0], "st_gate": [0, 0], "fh": [None, None]}

    def finish_pre(half, mpv):
        Ob = pA if half == 0 else pB
        Sb = pC if half == 0 else pD
        w("act", "s_pe", mpv)
        v1 = act(lambda E: E.copy(r_srow[half][:, :], Sb[0:1, :]))
        w("act", "s_dve", atst["ot_free"][half])
        v2 = act(lambda E: E.copy(o_t[half][:, :], Ob[:, :]))
        atst["o_ready"][half] = v2
        w("dve", "s_act", v1)
        v3 = dve(lambda E: E.reciprocal(r_rinv[half][:, :], r_srow[half][:, :]))
        w("dve", "s_dve", v3)
        v4 = dve(lambda E: E.tensor_copy(r_rinvb[half][:, :], r_rinv[half][:, :]))
        atst["fh"][half] = (v1, v2, v4)

    def finish_post(half, hh, jq):
        Sb = pC if half == 0 else pD
        v1, v2, v4 = atst["fh"][half]
        w("pe", "s_dve", v4)
        w("pe", "s_act", v1)
        v5 = pe(lambda E: E.matmul(Sb[:, :], onesr_b[:, :], r_rinvb[half][:, :], start=True, stop=True))
        w("dve", "s_pe", v5)
        w("dve", "s_act", v2)
        w("dve", "s_ob0" if half == 0 else "s_ob1", atst["ob_free"][half])
        v6 = dve(lambda E: E.tensor_tensor(o_b[half][:, :], o_t[half][:, :], Sb[:, :], ALU.mult))
        atst["sum_free"][half] = v6
        atst["ot_free"][half] = v6
        w("gp", "s_dve", v6)
        dest = jq if jq < 8 else 15 - jq
        coff = 0 if jq < 8 else CH
        atst["ob_free"][half] = gp_store(
            a2a_in.ap()[512 * dest + 128 * hh: 512 * dest + 128 * (hh + 1), coff:coff + CH],
            o_b[half][:, :], "s_ob0" if half == 0 else "s_ob1")

    q_gate = [0, 0]
    for hh in range(4):
        for i in range(8):
            qslot = (hh * 8 + i) % 2
            qsem = "s_qa" if qslot == 0 else "s_qb"
            w("sync", "s_pe", q_gate[qslot])
            vq = sy_load(q_t[qslot][:, :],
                         q_spill.rearrange("(h p) t -> p h t", p=128)[:, hh, 2 * i * CH:(2 * i + 2) * CH],
                         qsem)
            w("pe", qsem, vq)
            first = [True, True]
            sfirst = [True, True]
            its = [(kc, kt) for kc in range(2 * i + 2) for kt in range(KPC)]
            nits = len(its)
            exp_mark = [0] * nits
            gate_mark = [None] * nits

            def emit_st(idx):
                kc, kt = its[idx]
                b = idx % 2
                both = kc <= 2 * i
                kg = kc * KPC + kt
                w("pe", "s_act", atst["st_gate"][b])
                if both:
                    pe(lambda E, b=b, kg=kg, qs=qslot: E.matmul(
                        stX[b][:, 0:CH], k_sb[:, kg * 128:(kg + 1) * 128], q_t[qs][:, 0:CH],
                        start=True, stop=True), tick=False)
                mst = pe(lambda E, b=b, kg=kg, qs=qslot: E.matmul(
                    stX[b][:, CH:2 * CH], k_sb[:, kg * 128:(kg + 1) * 128], q_t[qs][:, CH:2 * CH],
                    start=True, stop=True))
                w("act", "s_pe", mst)
                if both:
                    vex = act(lambda E, b=b: E.activation(p_sb[b][:, :], stX[b][:, :], AF.Exp, scale=ISQ))
                else:
                    vex = act(lambda E, b=b: E.activation(p_sb[b][:, CH:2 * CH], stX[b][:, CH:2 * CH],
                                                          AF.Exp, scale=ISQ))
                atst["st_gate"][b] = vex
                exp_mark[idx] = vex
                gate = ("s_act", vex)
                if kc == 2 * i and both:
                    w("dve", "s_act", vex)
                    vm = dve(lambda E, b=b, kt=kt: E.tensor_tensor(
                        p_sb[b][:, 0:CH], p_sb[b][:, 0:CH], masks_b[:, kt, :], ALU.mult))
                    gate = ("s_dve", vm)
                if kc == 2 * i + 1:
                    w("dve", "s_act", vex)
                    vm = dve(lambda E, b=b, kt=kt: E.tensor_tensor(
                        p_sb[b][:, CH:2 * CH], p_sb[b][:, CH:2 * CH], masks_b[:, kt, :], ALU.mult))
                    gate = ("s_dve", vm)
                gate_mark[idx] = gate

            def emit_sumpv(idx):
                kc, kt = its[idx]
                b = idx % 2
                both = kc <= 2 * i
                kg = kc * KPC + kt
                diag0 = (kc == 2 * i)
                diag1 = (kc == 2 * i + 1)
                stop0 = diag0 and kt == KPC - 1
                stop1 = diag1 and kt == KPC - 1
                g = gate_mark[idx]
                w("pe", g[0], g[1])
                mpv0 = None
                if both:
                    if first[0]:
                        w("pe", "s_act", atst["o_ready"][0])
                    if sfirst[0]:
                        w("pe", "s_dve", atst["sum_free"][0])
                    pe(lambda E, b=b, s0=sfirst[0], z0=stop0: E.matmul(
                        pC[0:1, :], ones_b[:, :], p_sb[b][:, 0:CH], start=s0, stop=z0), tick=False)
                    mpv0 = pe(lambda E, b=b, kg=kg, f0=first[0], z0=stop0: E.matmul(
                        pA[:, :], v_sb[:, kg, :], p_sb[b][:, 0:CH], start=f0, stop=z0))
                    first[0] = False
                    sfirst[0] = False
                if first[1]:
                    w("pe", "s_act", atst["o_ready"][1])
                if sfirst[1]:
                    w("pe", "s_dve", atst["sum_free"][1])
                pe(lambda E, b=b, s1=sfirst[1], z1=stop1: E.matmul(
                    pD[0:1, :], ones_b[:, :], p_sb[b][:, CH:2 * CH], start=s1, stop=z1), tick=False)
                mpv1 = pe(lambda E, b=b, kg=kg, f1=first[1], z1=stop1: E.matmul(
                    pB[:, :], v_sb[:, kg, :], p_sb[b][:, CH:2 * CH], start=f1, stop=z1))
                first[1] = False
                sfirst[1] = False
                return mpv0, mpv1

            emit_st(0)
            for idx in range(nits):
                kc, kt = its[idx]
                mpv0, mpv1 = emit_sumpv(idx)
                fin0 = (kc == 2 * i and kt == KPC - 1)
                fin1 = (kc == 2 * i + 1 and kt == KPC - 1)
                if fin0:
                    finish_pre(0, mpv0)
                if idx + 1 < nits:
                    emit_st(idx + 1)
                if fin0:
                    finish_post(0, hh, 2 * i)
                if fin1:
                    finish_pre(1, mpv1)
                    finish_post(1, hh, 2 * i + 1)
            q_gate[qslot] = cur("s_pe")

    w("gp", "s_ob0", cur("s_ob0"))
    w("gp", "s_ob1", cur("s_ob1"))
    inc("s_cc")

    def _a2a(E):
        with allow_ici_all_to_all():
            return E.collective_compute(
                "AllToAll", ALU.bypass, replica_groups=[list(range(NCORES))],
                ins=[a2a_in.ap()], outs=[a2a_out.ap()])
    ops["gp"].append(lambda E: _a2a(E).then_inc(SEM["s_cc"]))

    fence(("pe", "act", "dve", "sync"))
    flush()
    p2.close()
    pkv.close()
    o_gat = a2a_out.ap().rearrange("(c p) t -> p c t", p=128)
    w("sync", "s_cc", 2)
    x1_r = x1_d.rearrange("(c p) t -> p c t", p=128)
    dn_r = dn_d.ap().rearrange("(c p) t -> p c t", p=128)
    out_r = out_fm.rearrange("(c p) t -> p c t", p=128)

    for qc in range(NCHK):
        cols = slice(qc * CH, (qc + 1) * CH)
        pq = ExitStack()
        la_g = pq.enter_context(nc.sbuf_tensor(f"la_g_{qc}", [128, CH], BF16))
        la_u = pq.enter_context(nc.sbuf_tensor(f"la_u_{qc}", [128, CH], BF16))
        lad_b = pq.enter_context(nc.sbuf_tensor(f"lad_b_{qc}", [128, CH], BF16))
        vz1 = dve(lambda E: E.memset(la_g[:, :], 0))
        vz2 = dve(lambda E: E.memset(la_u[:, :], 0))
        vz3 = dve(lambda E: E.memset(lad_b[:, :], 0))
        ln3b = pq.enter_context(nc.sbuf_tensor(f"ln3b_{qc}", [128, NDC, CH], BF16))

        p3 = ExitStack()
        o_sb = p3.enter_context(nc.sbuf_tensor(f"o_sb_{qc}", [128, NDC, CH], BF16))
        wo_sb = [p3.enter_context(nc.sbuf_tensor(f"wo_sb{i}_{qc}", [128, NDC, 128], BF16)) for i in range(2)]
        ap_sb = p3.enter_context(nc.sbuf_tensor(f"ap_sb_{qc}", [128, NDC, CH], BF16))
        sqb_t = p3.enter_context(nc.sbuf_tensor(f"sqb_t_{qc}", [128, CH], BF16))
        ab_t = p3.enter_context(nc.sbuf_tensor(f"ab_t_{qc}", [128, CH], BF16))
        x_t = [p3.enter_context(nc.sbuf_tensor(f"x_t{i}_{qc}", [128, CH], F32)) for i in range(2)]
        x1_t = [p3.enter_context(nc.sbuf_tensor(f"x1w_t{i}_{qc}", [128, CH], F32)) for i in range(2)]
        ap_t = p3.enter_context(nc.sbuf_tensor(f"ap_t_{qc}", [128, CH], F32))

        vo = sy_load(o_sb[:, :, :], o_gat[:, :, cols], "s_x")
        w("pe", "s_x", vo)

        wo_gate = [0, 0]
        sq_gate = 0
        last_stat = 0
        for dm in range(NDC):
            slot = dm % 2
            sname = "s_woa" if slot == 0 else "s_wob"
            w("gp", "s_pe", wo_gate[slot])
            vw = gp_load(wo_sb[slot][:, :, :], wo_t[dm].rearrange("(c p) n -> p c n", p=128), sname)
            w("pe", sname, vw)
            bank = pA if slot == 0 else pB
            for f in range(NDC):
                mq = pe(lambda E, f=f, s=slot, bk=bank: E.matmul(
                    bk[:, :], wo_sb[s][:, f, :], o_sb[:, f, :],
                    start=(f == 0), stop=(f == NDC - 1)), tick=(f == NDC - 1))
            wo_gate[slot] = mq
            w("act", "s_pe", mq)
            va1 = act(lambda E, dm=dm, bk=bank: E.copy(ap_sb[:, dm, :], bk[:, :]))
            w("pe", "s_act", va1)
            w("dve", "s_act", va1)
            w("dve", "s_pe", sq_gate)
            vsq = dve(lambda E, dm=dm: E.tensor_mul(sqb_t[:, :], ap_sb[:, dm, :], ap_sb[:, dm, :]))
            w("pe", "s_dve", vsq)
            pe(lambda E, dm=dm: E.matmul(pC[0:1, :], ones_b[:, :], ap_sb[:, dm, :],
                                         start=(dm == 0), stop=(dm == NDC - 1)), tick=False)
            last_stat = pe(lambda E, dm=dm: E.matmul(pD[0:1, :], ones_b[:, :], sqb_t[:, :],
                                                     start=(dm == 0), stop=(dm == NDC - 1)))
            sq_gate = last_stat

        bdone, stat_free = ln_rows(pC, pD, last_stat)
        w("dve", "s_pe", bdone)
        xt_gate = [0, 0]
        x1s_mark = [0] * NDC
        va2 = 0
        ln3_gate = 0
        for dm in range(NDC):
            slot = dm % 2
            sname = "s_xta" if slot == 0 else "s_xtb"
            ssem = "s_1sa" if slot == 0 else "s_1sb"
            w("sync", "s_dve", xt_gate[slot])
            vx = sy_load(x_t[slot][:, :], x_r[:, dm, cols], sname)
            if dm > 1:
                w("dve", ssem, x1s_mark[dm - 2])
            if dm > 0:
                w("dve", "s_dve", va2)
            t1 = dve(lambda E, dm=dm: E.scalar_tensor_tensor(
                ap_t[:, :], ap_sb[:, dm, :], 1.0, mean_t, ALU.mult, ALU.subtract))
            w("dve", "s_dve", t1)
            t2 = dve(lambda E: E.tensor_tensor(ap_t[:, :], ap_t[:, :], rstd_t, ALU.mult))
            w("dve", sname, vx)
            w("dve", "s_dve", t2)
            va2 = dve(lambda E, dm=dm, s=slot: E.tensor_tensor(
                x1_t[s][:, :], ap_t[:, :], x_t[s][:, :], ALU.add))
            xt_gate[slot] = va2
            w("dve", "s_dve", va2)
            w("dve", "s_pe", ln3_gate)
            vc1 = dve(lambda E, s=slot: E.tensor_copy(ab_t[:, :], x1_t[s][:, :]))
            w("dve", "s_dve", vc1)
            vc2 = dve(lambda E: E.tensor_mul(sqb_t[:, :], ab_t[:, :], ab_t[:, :]))
            w("pe", "s_dve", vc2)
            if dm == 0:
                w("pe", "s_act", stat_free)
            pe(lambda E, dm=dm: E.matmul(pC[0:1, :], ones_b[:, :], ab_t[:, :],
                                         start=(dm == 0), stop=(dm == NDC - 1)), tick=False)
            last_stat = pe(lambda E, dm=dm: E.matmul(pD[0:1, :], ones_b[:, :], sqb_t[:, :],
                                                     start=(dm == 0), stop=(dm == NDC - 1)))
            ln3_gate = last_stat
            w("gp", "s_dve", va2)
            x1s_mark[dm] = gp_store(x1_r[:, dm, cols], x1_t[slot][:, :], ssem)

        bdone, stat_free = ln_rows(pC, pD, last_stat, stA_gate=va2)
        w("dve", "s_pe", bdone)
        x1l_gate = list(xt_gate)
        vb3 = 0
        for dm in range(NDC):
            slot = dm % 2
            lsem = "s_x1ta" if slot == 0 else "s_x1tb"
            ssem = "s_1sa" if slot == 0 else "s_1sb"
            w("sync", ssem, x1s_mark[dm])
            w("sync", "s_dve", x1l_gate[slot])
            vxl = sy_load(x_t[slot][:, :], x1_r[:, dm, cols], lsem)
            w("dve", lsem, vxl)
            t1 = dve(lambda E, dm=dm, s=slot: E.scalar_tensor_tensor(
                ln3b[:, dm, :], x_t[s][:, :], 1.0, mean_t, ALU.mult, ALU.subtract))
            w("dve", "s_dve", t1)
            vb3 = dve(lambda E, dm=dm: E.tensor_tensor(ln3b[:, dm, :], ln3b[:, dm, :], rstd_t, ALU.mult))
            x1l_gate[slot] = vb3
        w("pe", "s_dve", vb3)

        mla = 0
        for dc in range(NDC):
            if dc == 0:
                w("pe", "s_act", stat_free)
            pe(lambda E, dc=dc: E.matmul(pC[0:RANK, :], ga_sb[:, dc, :], ln3b[:, dc, :],
                                         start=(dc == 0), stop=(dc == NDC - 1)), tick=False)
            mla = pe(lambda E, dc=dc: E.matmul(pD[0:RANK, :], ua_sb[:, dc, :], ln3b[:, dc, :],
                                               start=(dc == 0), stop=(dc == NDC - 1)))
        w("act", "s_pe", mla)
        w("act", "s_dve", vz2)
        act(lambda E: E.copy(la_g[0:RANK, :], pC[0:RANK, :]))
        vlu = act(lambda E: E.copy(la_u[0:RANK, :], pD[0:RANK, :]))
        w("pe", "s_act", vlu)

        fence()
        flush()
        p3.close()

        ph = ExitStack()
        h_sb = ph.enter_context(nc.sbuf_tensor(f"h_sb_{qc}", [128, NHC, CH], BF16))
        p4a = ExitStack()
        gw_sb = [p4a.enter_context(nc.sbuf_tensor(f"gw_sb{i}_{qc}", [128, NDC + 1, 128], BF16)) for i in range(2)]
        uw_sb = [p4a.enter_context(nc.sbuf_tensor(f"uw_sb{i}_{qc}", [128, NDC + 1, 128], BF16)) for i in range(2)]
        s_t = p4a.enter_context(nc.sbuf_tensor(f"s_t_{qc}", [128, CH], BF16))
        u_t = p4a.enter_context(nc.sbuf_tensor(f"u_t_{qc}", [128, CH], BF16))

        gu_gate = [0, 0]
        bank_gate = [0, 0]
        sil_gate = 0
        vh = 0
        mld = 0
        for hc in range(NHC):
            slot = hc % 2
            sname = "s_gua" if slot == 0 else "s_gub"
            w("gp", "s_pe", gu_gate[slot])
            gp_load(gw_sb[slot][:, :, :], gw_t[hc].rearrange("(c p) n -> p c n", p=128), sname)
            vwu = gp_load(uw_sb[slot][:, :, :], uw_t[hc].rearrange("(c p) n -> p c n", p=128), sname)
            w("pe", sname, vwu)
            bank = stX[slot]
            w("pe", "s_act", bank_gate[slot])
            for dc in range(NDC):
                pe(lambda E, dc=dc, s=slot, bk=bank: E.matmul(
                    bk[:, 0:CH], gw_sb[s][:, dc, :], ln3b[:, dc, :],
                    start=(dc == 0), stop=False), tick=False)
            pe(lambda E, s=slot, bk=bank: E.matmul(
                bk[:, 0:CH], gw_sb[s][:, NDC, :], la_g[:, :],
                start=False, stop=True), tick=False)
            for dc in range(NDC):
                pe(lambda E, dc=dc, s=slot, bk=bank: E.matmul(
                    bk[:, CH:2 * CH], uw_sb[s][:, dc, :], ln3b[:, dc, :],
                    start=(dc == 0), stop=False), tick=False)
            mu = pe(lambda E, s=slot, bk=bank: E.matmul(
                bk[:, CH:2 * CH], uw_sb[s][:, NDC, :], la_u[:, :],
                start=False, stop=True))
            gu_gate[slot] = mu
            w("act", "s_pe", mu)
            w("act", "s_dve", sil_gate)
            act(lambda E, bk=bank: E.activation(s_t[:, :], bk[:, 0:CH], AF.Silu))
            vu_ = act(lambda E, bk=bank: E.copy(u_t[:, :], bk[:, CH:2 * CH]))
            bank_gate[slot] = vu_
            w("dve", "s_act", vu_)
            vh = dve(lambda E, hc=hc: E.tensor_tensor(h_sb[:, hc, :], s_t[:, :], u_t[:, :], ALU.mult))
            sil_gate = vh
            w("pe", "s_dve", vh)
            if hc == 0:
                w("pe", "s_act", vlu)
            mld = pe(lambda E, hc=hc: E.matmul(pA[0:RANK, :], da_sb[:, hc, :], h_sb[:, hc, :],
                                               start=(hc == 0), stop=(hc == NHC - 1)))
        w("act", "s_pe", mld)
        w("act", "s_dve", vz3)
        vle = act(lambda E: E.copy(lad_b[0:RANK, :], pA[0:RANK, :]))
        w("pe", "s_act", vle)

        fence()
        flush()
        p4a.close()

        p4b = ExitStack()
        dwh = [p4b.enter_context(nc.sbuf_tensor(f"dwh{i}_{qc}", [128, 44, 128], BF16)) for i in range(2)]
        dn_t2 = [p4b.enter_context(nc.sbuf_tensor(f"dn_t2{i}_{qc}", [128, CH], BF16)) for i in range(2)]
        sqd_t = p4b.enter_context(nc.sbuf_tensor(f"sqd_t_{qc}", [128, CH], BF16))
        x1_t4 = [p4b.enter_context(nc.sbuf_tensor(f"x1r_t{i}_{qc}", [128, CH], F32)) for i in range(2)]
        out_t = [p4b.enter_context(nc.sbuf_tensor(f"out_t{i}_{qc}", [128, CH], F32)) for i in range(2)]
        dnl_t = [p4b.enter_context(nc.sbuf_tensor(f"dnl_t{i}_{qc}", [128, CH], BF16)) for i in range(2)]
        dnf_t = p4b.enter_context(nc.sbuf_tensor(f"dnf_t_{qc}", [128, CH], F32))

        dw_gate = [0, 0]
        sq_gate = 0
        last_stat = 0
        dns_mark = [0] * NDC
        dnt_gate = [0, 0]
        HALF0, HALF1 = 44, 43
        for dm in range(NDC):
            bank = pB if dm % 2 == 0 else pC
            mdn = 0
            for half in range(2):
                sname = "s_dwa" if half == 0 else "s_dwb"
                nch = HALF0 if half == 0 else HALF1
                base = 0 if half == 0 else HALF0
                w("gp", "s_pe", dw_gate[half])
                vw = gp_load(dwh[half][:, 0:nch, :],
                             dw_t[dm][base * 128:(base + nch) * 128, :].rearrange("(c p) n -> p c n", p=128),
                             sname)
                w("pe", sname, vw)
                for lc in range(nch):
                    hc = base + lc
                    rhs_h = (lambda E, hc=hc: h_sb[:, hc, :]) if hc < NHC else (lambda E: lad_b[:, :])
                    mdn = pe(lambda E, lc=lc, half=half, bk=bank, rh=rhs_h, last=(hc == NHC): E.matmul(
                        bk[:, :], dwh[half][:, lc, :], rh(E),
                        start=(half == 0 and lc == 0), stop=last), tick=(lc == nch - 1))
                dw_gate[half] = mdn
            slot = dm % 2
            dsem = "s_dsa" if slot == 0 else "s_dsb"
            w("act", "s_pe", mdn)
            if dm > 1:
                w("act", dsem, dns_mark[dm - 2])
            w("act", "s_dve", dnt_gate[slot])
            va1 = act(lambda E, bk=bank, s=slot: E.copy(dn_t2[s][:, :], bk[:, :]))
            w("pe", "s_act", va1)
            w("dve", "s_act", va1)
            w("dve", "s_pe", sq_gate)
            vsq = dve(lambda E, s=slot: E.tensor_mul(sqd_t[:, :], dn_t2[s][:, :], dn_t2[s][:, :]))
            dnt_gate[slot] = vsq
            w("pe", "s_dve", vsq)
            pe(lambda E, dm=dm, s=slot: E.matmul(pD[0:1, :], ones_b[:, :], dn_t2[s][:, :],
                                                 start=(dm == 0), stop=(dm == NDC - 1)), tick=False)
            last_stat = pe(lambda E, dm=dm: E.matmul(pA[0:1, :], ones_b[:, :], sqd_t[:, :],
                                                     start=(dm == 0), stop=(dm == NDC - 1)))
            sq_gate = last_stat
            w("gp", "s_act", va1)
            dns_mark[dm] = gp_store(dn_r[:, dm, cols], dn_t2[slot][:, :], dsem)

        bdone, stat_free = ln_rows(pD, pA, last_stat)
        w("dve", "s_pe", bdone)
        x1t_gate = [0, 0]
        ot_gate = [0, 0]
        dnl_gate = [0, 0]
        vo3 = 0
        for dm in range(NDC):
            slot = dm % 2
            lsem = "s_x1ta" if slot == 0 else "s_x1tb"
            dlsem = "s_dla" if slot == 0 else "s_dlb"
            dsem = "s_dsa" if slot == 0 else "s_dsb"
            w("sync", "s_dve", x1t_gate[slot])
            vx1 = sy_load(x1_t4[slot][:, :], x1_r[:, dm, cols], lsem)
            w("sync", dsem, dns_mark[dm])
            w("sync", "s_dve", dnl_gate[slot])
            vdl = sy_load(dnl_t[slot][:, :], dn_r[:, dm, cols], dlsem)
            if dm > 0:
                w("dve", "s_dve", vo3)
            w("dve", dlsem, vdl)
            t1 = dve(lambda E, s=slot: E.scalar_tensor_tensor(
                dnf_t[:, :], dnl_t[s][:, :], 1.0, mean_t, ALU.mult, ALU.subtract))
            dnl_gate[slot] = t1
            w("dve", "s_dve", t1)
            t2 = dve(lambda E: E.tensor_tensor(dnf_t[:, :], dnf_t[:, :], rstd_t, ALU.mult))
            w("dve", lsem, vx1)
            w("dve", "s_dve", t2)
            w("dve", "s_ot0" if slot == 0 else "s_ot1", ot_gate[slot])
            vo3 = dve(lambda E, s=slot: E.tensor_tensor(
                out_t[s][:, :], dnf_t[:, :], x1_t4[s][:, :], ALU.add))
            x1t_gate[slot] = vo3
            w("sync", "s_dve", vo3)
            ot_gate[slot] = sy_store(out_r[:, dm, cols], out_t[slot][:, :], "s_ot0" if slot == 0 else "s_ot1")
        w("pe", "s_dve", vo3)

        fence()
        flush()
        p4b.close()
        ph.close()
        pq.close()

    w("sync", "s_ot0", cur("s_ot0"))
    w("sync", "s_ot1", cur("s_ot1"))
    w("gp", "s_gst", cur("s_gst"))
    flush()
    st.close()
    return nc


def _stage(inputs):
    S = int(np.asarray(inputs["L0_attn_hidden_states"]).shape[1])
    CH = S // 16
    TPC = S // 8
    KPC = CH // 128
    hid = np.ascontiguousarray(np.asarray(inputs["L0_attn_hidden_states"], dtype=np.float32)[0])
    wq = np.asarray(inputs["L0_attn_w_q"], dtype=np.float32)
    wk = np.asarray(inputs["L0_attn_w_k"], dtype=np.float32)
    wv = np.asarray(inputs["L0_attn_w_v"], dtype=np.float32)
    wo = np.asarray(inputs["L0_attn_w_o"], dtype=np.float32)
    gw = np.asarray(inputs["L0_ffn_gate_w"], dtype=np.float32)
    uw = np.asarray(inputs["L0_ffn_up_w"], dtype=np.float32)
    dw = np.asarray(inputs["L0_ffn_down_w"], dtype=np.float32)
    ga = np.asarray(inputs["L0_ffn_gate_a"], dtype=np.float32)
    gb = np.asarray(inputs["L0_ffn_gate_b"], dtype=np.float32)
    ua = np.asarray(inputs["L0_ffn_up_a"], dtype=np.float32)
    ub = np.asarray(inputs["L0_ffn_up_b"], dtype=np.float32)
    da = np.asarray(inputs["L0_ffn_down_a"], dtype=np.float32)
    db = np.asarray(inputs["L0_ffn_down_b"], dtype=np.float32)

    def pad_strips(wT, b, nstr):
        K = wT.shape[0]
        out = np.zeros((nstr, K + 128, 128), np.float32)
        for i in range(nstr):
            out[i, :K, :] = wT[:, i * 128:(i + 1) * 128]
            out[i, K:K + RANK, :] = b[i * 128:(i + 1) * 128, :].T
        return np.ascontiguousarray(out)

    shared = {
        "wo_t": np.ascontiguousarray(wo.reshape(DM, NDC, 128).transpose(1, 0, 2)),
        "gw_t": pad_strips(gw.T, gb, NHC),
        "uw_t": pad_strips(uw.T, ub, NHC),
        "dw_t": pad_strips(dw.T, db, NDC),
        "ga_t": np.ascontiguousarray(ga.T),
        "ua_t": np.ascontiguousarray(ua.T),
        "da_t": np.ascontiguousarray(da.T),
        "ones_i": np.ones((128, 1), np.float32),
        "onesr_i": np.ones((1, 128), np.float32),
    }
    k_idx = np.arange(128)[:, None]
    q_idx = np.arange(CH)[None, :]
    masks = np.empty((128, KPC, CH), np.float32)
    for kt in range(KPC):
        masks[:, kt, :] = ((kt * 128 + k_idx) <= q_idx).astype(np.float32)
    shared["masks_i"] = masks

    in_maps = []
    for c in range(NCORES):
        toks = np.concatenate([np.arange(c * CH, (c + 1) * CH),
                               np.arange((15 - c) * CH, (16 - c) * CH)])
        m = dict(shared)
        m["x_fm"] = np.ascontiguousarray(hid[toks].T)
        m["wq_t"] = np.ascontiguousarray(
            wq[:, 512 * c:512 * (c + 1)].reshape(DM, 4, 128).transpose(1, 0, 2))
        m["wk_t"] = np.ascontiguousarray(wk[:, 128 * c:128 * (c + 1)])
        m["wv_t"] = np.ascontiguousarray(wv[:, 128 * c:128 * (c + 1)])
        in_maps.append(m)
    return S, CH, in_maps


def kernel(**inputs):
    S, CH, in_maps = _stage(inputs)
    nc = build(S)
    res = run_bass_kernel_spmd(nc, in_maps, core_ids=list(range(NCORES)))
    out = np.empty((1, S, DM), np.float32)
    for c in range(NCORES):
        of = res.results[c]["out_fm"]
        out[0, c * CH:(c + 1) * CH] = of[:, :CH].T
        out[0, (15 - c) * CH:(16 - c) * CH] = of[:, CH:].T
    return out


# version 05
# speedup vs baseline: 1.0784x; 1.0784x over previous
import math
import numpy as np
from contextlib import ExitStack

import concourse.bass as bass
import concourse.mybir as mybir
from concourse.bass_utils import run_bass_kernel_spmd
from concourse.aot_env import allow_ici_all_to_all

F32 = mybir.dt.float32
BF16 = mybir.dt.bfloat16
AF = mybir.ActivationFunctionType
ALU = mybir.AluOpType

NCORES = 8
NH, HKV, G, HD = 32, 8, 4, 128
DM, DH, RANK = 4096, 11008, 16
NDC = DM // 128
NHC = DH // 128
NHH = DH // 2
EPS = 1e-5
ISQ = 1.0 / math.sqrt(HD)
ENGS = ("pe", "act", "dve", "sync", "gp")
SEC = {"pe": "tensor", "act": "scalar", "dve": "vector", "sync": "sync", "gp": "gpsimd"}

SEM_NAMES = ["s_gw", "s_cc", "s_x", "s_of", "s_lga", "s_lgb", "s_qa", "s_qb",
             "s_woa", "s_wob", "s_gua", "s_gub", "s_dwa", "s_dwb",
             "s_xta", "s_xtb", "s_x1ta", "s_x1tb", "s_st", "s_gst",
             "s_qe0", "s_qe1", "s_ob0", "s_ob1", "s_ot0", "s_ot1",
             "s_1sa", "s_1sb", "s_dsa", "s_dsb", "s_dla", "s_dlb",
             "s_pe", "s_act", "s_dve"]


def build(S):
    CH = S // 16
    TPC = S // 8
    KPC = CH // 128
    NKT = S // 128
    assert CH % 128 == 0 and 128 <= CH <= 512

    nc = bass.Bass("TRN2", target_bir_lowering=False, debug=False)

    def din(n, sh):
        return nc.dram_tensor(n, sh, F32, kind="ExternalInput").ap()

    x_fm = din("x_fm", [DM, TPC])
    wq_t = din("wq_t", [4, 128, NDC, 128])
    wk_t = din("wk_t", [128, NDC, 128])
    wv_t = din("wv_t", [128, NDC, 128])
    wo_t = din("wo_t", [NDC, 128, NDC, 128])
    gw_t = din("gw_t", [NHC, 128, NDC + 1, 128])
    uw_t = din("uw_t", [NHC, 128, NDC + 1, 128])
    dw_t = din("dw_t", [NDC, 128, 87, 128])
    ga_t = din("ga_t", [DM, RANK])
    ua_t = din("ua_t", [DM, RANK])
    da_t = din("da_t", [DH, RANK])
    masks_i = din("masks_i", [128, KPC, CH])
    ones_i = din("ones_i", [128, 1])
    onesr_i = din("onesr_i", [1, 128])

    out_fm = nc.dram_tensor("out_fm", [DM, TPC], F32, kind="ExternalOutput").ap()

    ln_shard = nc.dram_tensor("ln_shard", [DM, TPC], BF16)
    ln_gat = nc.dram_tensor("ln_gat", [NCORES * DM, TPC], BF16)
    q_spill = nc.dram_tensor("q_spill", [512, S], BF16).ap()
    a2a_in = nc.dram_tensor("a2a_in", [NCORES * 512, TPC], BF16)
    a2a_out = nc.dram_tensor("a2a_out", [NCORES * 512, TPC], BF16)
    x1_d = nc.dram_tensor("x1_d", [DM, TPC], F32).ap()
    dn_d = nc.dram_tensor("dn_d", [DM, TPC], BF16)

    ops = {e: [] for e in ENGS}
    cnt = {}
    SEM = {}

    def inc(sem, k=1):
        cnt[sem] = cnt.get(sem, 0) + k
        return cnt[sem]

    def cur(sem):
        return cnt.get(sem, 0)

    def w(eng, sem, val):
        if val and val > 0:
            ops[eng].append(lambda E, s=sem, v=val: E.wait_ge(SEM[s], v))

    def gp_load(dst, src, sem):
        v = inc(sem, 16)
        ops["gp"].append(lambda E, o=dst, i=src, s=sem: E.dma_start(out=o, in_=i).then_inc(SEM[s], 16))
        return v

    def sy_load(dst, src, sem):
        v = inc(sem, 16)
        ops["sync"].append(lambda E, o=dst, i=src, s=sem: E.dma_start(out=o, in_=i).then_inc(SEM[s], 16))
        return v

    def sy_store(dst, src, sem="s_st"):
        v = inc(sem, 16)
        ops["sync"].append(lambda E, o=dst, i=src, s=sem: E.dma_start(out=o, in_=i).then_inc(SEM[s], 16))
        return v

    def gp_store(dst, src, sem="s_gst"):
        v = inc(sem, 16)
        ops["gp"].append(lambda E, o=dst, i=src, s=sem: E.dma_start(out=o, in_=i).then_inc(SEM[s], 16))
        return v

    def pe(fn, tick=True):
        if tick:
            v = inc("s_pe")
            ops["pe"].append(lambda E, f=fn: f(E).then_inc(SEM["s_pe"], 1))
            return v
        ops["pe"].append(lambda E, f=fn: f(E))
        return None

    def act(fn):
        v = inc("s_act")
        ops["act"].append(lambda E, f=fn: f(E).then_inc(SEM["s_act"], 1))
        return v

    def dve(fn):
        v = inc("s_dve")
        ops["dve"].append(lambda E, f=fn: f(E).then_inc(SEM["s_dve"], 1))
        return v

    STORE_SEMS = ("s_gst", "s_st", "s_qe0", "s_qe1", "s_ob0", "s_ob1", "s_ot0", "s_ot1", "s_1sa", "s_1sb", "s_dsa", "s_dsb")

    def fence(engines=ENGS, sems=("s_pe", "s_act", "s_dve")):
        marks = {k: cur(k) for k in sems + STORE_SEMS}
        for e in engines:
            for sname, v in marks.items():
                if v:
                    w(e, sname, v)

    st = ExitStack()

    ones_b = st.enter_context(nc.sbuf_tensor("ones_b", [128, 1], BF16))
    onesr_f = st.enter_context(nc.sbuf_tensor("onesr_f", [1, 128], F32))
    onesr_b = st.enter_context(nc.sbuf_tensor("onesr_b", [1, 128], BF16))
    ga_sb = st.enter_context(nc.sbuf_tensor("ga_sb", [128, NDC, RANK], BF16))
    ua_sb = st.enter_context(nc.sbuf_tensor("ua_sb", [128, NDC, RANK], BF16))
    da_sb = st.enter_context(nc.sbuf_tensor("da_sb", [128, NHC, RANK], BF16))
    r_sum = st.enter_context(nc.sbuf_tensor("r_sum", [1, CH], F32))
    r_sq = st.enter_context(nc.sbuf_tensor("r_sq", [1, CH], F32))
    r_mean = st.enter_context(nc.sbuf_tensor("r_mean", [1, CH], F32))
    r_var = st.enter_context(nc.sbuf_tensor("r_var", [1, CH], F32))
    r_tmp = st.enter_context(nc.sbuf_tensor("r_tmp", [1, CH], F32))
    r_rstd = st.enter_context(nc.sbuf_tensor("r_rstd", [1, CH], F32))
    r_srow = [st.enter_context(nc.sbuf_tensor(f"r_srow{i}", [1, CH], F32)) for i in range(2)]
    r_rinv = [st.enter_context(nc.sbuf_tensor(f"r_rinv{i}", [1, CH], F32)) for i in range(2)]
    r_rinvb = [st.enter_context(nc.sbuf_tensor(f"r_rinvb{i}", [1, CH], BF16)) for i in range(2)]

    stA = st.enter_context(nc.psum_tensor("stA", [128, 2 * CH], F32))
    stB = st.enter_context(nc.psum_tensor("stB", [128, 2 * CH], F32))
    pA = st.enter_context(nc.psum_tensor("pA", [128, CH], F32))
    pB = st.enter_context(nc.psum_tensor("pB", [128, CH], F32))
    pC = st.enter_context(nc.psum_tensor("pC", [128, CH], F32))
    pD = st.enter_context(nc.psum_tensor("pD", [128, CH], F32))

    for n in SEM_NAMES:
        SEM[n] = st.enter_context(nc.semaphore(n))
    block = st.enter_context(nc.Block())

    def flush():
        for e in ENGS:
            if ops[e]:
                fns = ops[e]
                ops[e] = []

                def run(E, fns=fns):
                    for f in fns:
                        f(E)
                getattr(block, SEC[e])(run)

    mean_bc = bass.AP(stA, 0, [[2 * CH, 128], [0, NDC], [1, CH]])
    rstd_bc = bass.AP(stA, CH, [[2 * CH, 128], [0, NDC], [1, CH]])
    mean_t = bass.AP(stA, 0, [[2 * CH, 128], [1, CH]])
    rstd_t = bass.AP(stA, CH, [[2 * CH, 128], [1, CH]])

    def ln_rows(sum_bank, sq_bank, pe_gate, stA_gate=0):
        w("act", "s_pe", pe_gate)
        v1 = act(lambda E: E.copy(r_sum[:, :], sum_bank[0:1, 0:CH]))
        v2 = act(lambda E: E.copy(r_sq[:, :], sq_bank[0:1, 0:CH]))
        w("dve", "s_act", v2)
        a1 = dve(lambda E: E.tensor_scalar_mul(r_mean[:, :], r_sum[:, :], 1.0 / DM))
        w("dve", "s_dve", a1)
        a2 = dve(lambda E: E.tensor_scalar_mul(r_sq[:, :], r_sq[:, :], 1.0 / DM))
        w("dve", "s_dve", a2)
        a3 = dve(lambda E: E.tensor_mul(r_tmp[:, :], r_mean[:, :], r_mean[:, :]))
        w("dve", "s_dve", a3)
        a4 = dve(lambda E: E.tensor_sub(r_var[:, :], r_sq[:, :], r_tmp[:, :]))
        w("dve", "s_dve", a4)
        a4b = dve(lambda E: E.tensor_scalar_add(r_var[:, :], r_var[:, :], EPS))
        w("act", "s_dve", a4b)
        a5 = act(lambda E: E.activation(r_tmp[:, :], r_var[:, :], AF.Ln))
        w("act", "s_act", a5)
        a6 = act(lambda E: E.activation(r_rstd[:, :], r_tmp[:, :], AF.Exp, scale=-0.5))
        w("pe", "s_act", a6)
        if stA_gate:
            w("pe", "s_dve", stA_gate)
        pe(lambda E: E.matmul(stA[:, 0:CH], onesr_f[:, :], r_mean[:, :], start=True, stop=True), tick=False)
        b2 = pe(lambda E: E.matmul(stA[:, CH:2 * CH], onesr_f[:, :], r_rstd[:, :], start=True, stop=True))
        return b2, v2

    gp_load(ga_sb[:, :, :], ga_t.rearrange("(c p) r -> p c r", p=128), "s_gw")
    gp_load(ua_sb[:, :, :], ua_t.rearrange("(c p) r -> p c r", p=128), "s_gw")
    gp_load(da_sb[:, :, :], da_t.rearrange("(c p) r -> p c r", p=128), "s_gw")
    gp_load(ones_b[:, :], ones_i, "s_gw")
    v_ob = gp_load(onesr_b[:, :], onesr_i, "s_gw")
    v_of = sy_load(onesr_f[:, :], onesr_i, "s_of")

    p1 = ExitStack()
    x_sb = p1.enter_context(nc.sbuf_tensor("x_sb", [128, NDC, CH], F32))
    ln1b = p1.enter_context(nc.sbuf_tensor("ln1b", [128, NDC, CH], BF16))
    xb_t = p1.enter_context(nc.sbuf_tensor("xb_t", [128, CH], BF16))
    sq_t = p1.enter_context(nc.sbuf_tensor("sq_t", [128, CH], BF16))

    x_r = x_fm.rearrange("(c p) t -> p c t", p=128)
    NCHK = TPC // CH
    vb = 0
    stat_free = 0
    for cc in range(NCHK):
        vx = sy_load(x_sb[:, :, :], x_r[:, :, cc * CH:(cc + 1) * CH], "s_x")
        w("dve", "s_x", vx)
        m2 = 0
        for dc in range(NDC):
            if dc > 0:
                w("dve", "s_pe", m2)
            vc = dve(lambda E, dc=dc: E.tensor_copy(xb_t[:, :], x_sb[:, dc, :]))
            w("dve", "s_dve", vc)
            vs = dve(lambda E: E.tensor_mul(sq_t[:, :], xb_t[:, :], xb_t[:, :]))
            w("pe", "s_dve", vs)
            if cc == 0 and dc == 0:
                w("pe", "s_gw", v_ob)
                w("pe", "s_of", v_of)
            if dc == 0 and stat_free:
                w("pe", "s_act", stat_free)
            pe(lambda E, dc=dc: E.matmul(pC[0:1, :], ones_b[:, :], xb_t[:, :],
                                         start=(dc == 0), stop=(dc == NDC - 1)), tick=False)
            m2 = pe(lambda E, dc=dc: E.matmul(pD[0:1, :], ones_b[:, :], sq_t[:, :],
                                              start=(dc == 0), stop=(dc == NDC - 1)))
        bdone, stat_free = ln_rows(pC, pD, m2, stA_gate=vb)
        w("dve", "s_pe", bdone)
        if cc > 0:
            w("dve", "s_gst", cur("s_gst"))
        lsl = ln1b[:, :, :]
        va = dve(lambda E, l=lsl: E.scalar_tensor_tensor(
            l, x_sb[:, :, :], 1.0, mean_bc, ALU.mult, ALU.subtract))
        w("dve", "s_dve", va)
        vb = dve(lambda E, l=lsl: E.tensor_tensor(l, l, rstd_bc, ALU.mult))
        w("gp", "s_dve", vb)
        gp_store(ln_shard.ap().rearrange("(c p) t -> p c t", p=128)[:, :, cc * CH:(cc + 1) * CH],
                 ln1b[:, :, :])
        if cc + 1 < NCHK:
            w("sync", "s_dve", vb)

    w("gp", "s_gst", cur("s_gst"))
    inc("s_cc")
    ops["gp"].append(lambda E: E.collective_compute(
        "AllGather", ALU.bypass, replica_groups=[list(range(NCORES))],
        ins=[ln_shard.ap()], outs=[ln_gat.ap()]).then_inc(SEM["s_cc"]))

    fence(("pe", "act", "dve"))
    flush()
    p1.close()

    pkv = ExitStack()
    masks_b = pkv.enter_context(nc.sbuf_tensor("masks_b", [128, KPC, CH], BF16))
    k_sb = pkv.enter_context(nc.sbuf_tensor("k_sb", [128, S], BF16))
    v_sb = pkv.enter_context(nc.sbuf_tensor("v_sb", [128, NKT, 128], BF16))

    p1c = ExitStack()
    wq_sb = p1c.enter_context(nc.sbuf_tensor("wq_sb", [128, 4, NDC, 128], BF16))
    wk_sb = p1c.enter_context(nc.sbuf_tensor("wk_sb", [128, NDC, 128], BF16))
    wv_sb = p1c.enter_context(nc.sbuf_tensor("wv_sb", [128, NDC, 128], BF16))
    lg_sb = [p1c.enter_context(nc.sbuf_tensor(f"lg_sb{i}", [128, NDC, CH], BF16)) for i in range(2)]
    q_ev = [p1c.enter_context(nc.sbuf_tensor(f"q_ev{i}", [128, CH], BF16)) for i in range(2)]

    gp_load(masks_b[:, :, :], masks_i, "s_gw")
    for _hh in range(4):
        gp_load(wq_sb[:, _hh, :, :], wq_t[_hh], "s_gw")
    gp_load(wk_sb[:, :, :], wk_t, "s_gw")
    vwv = gp_load(wv_sb[:, :, :], wv_t, "s_gw")
    w("pe", "s_gw", vwv)
    w("sync", "s_cc", 1)

    def lg_src(j):
        r, off = (j, 0) if j < 8 else (15 - j, 1)
        return ln_gat.ap()[DM * r: DM * (r + 1), off * CH:(off + 1) * CH] \
            .rearrange("(c p) t -> p c t", p=128)

    qbanks = [pA, pB, pC, pD]
    lg_gate = [0, 0]
    qev_gate = [0, 0]
    for j in range(16):
        slot = j % 2
        sname = "s_lga" if slot == 0 else "s_lgb"
        w("sync", "s_pe", lg_gate[slot])
        vl = sy_load(lg_sb[slot][:, :, :], lg_src(j), sname)
        w("pe", sname, vl)
        for hh in range(4):
            for dc in range(NDC):
                m = pe(lambda E, hh=hh, dc=dc, s=slot: E.matmul(
                    qbanks[hh][:, :], wq_sb[:, hh, dc, :], lg_sb[s][:, dc, :],
                    start=(dc == 0), stop=(dc == NDC - 1)), tick=(dc == NDC - 1))
            w("act", "s_pe", m)
            w("act", "s_qe0" if hh % 2 == 0 else "s_qe1", qev_gate[hh % 2])
            ve = act(lambda E, hh=hh: E.copy(q_ev[hh % 2][:, :], qbanks[hh][:, :]))
            w("pe", "s_act", ve)
            w("gp", "s_act", ve)
            qev_gate[hh % 2] = gp_store(
                q_spill.rearrange("(h p) t -> p h t", p=128)[:, hh, j * CH:(j + 1) * CH],
                q_ev[hh % 2][:, :], "s_qe0" if hh % 2 == 0 else "s_qe1")
        for dc in range(NDC):
            m = pe(lambda E, dc=dc, s=slot: E.matmul(
                stA[:, 0:CH], wk_sb[:, dc, :], lg_sb[s][:, dc, :],
                start=(dc == 0), stop=(dc == NDC - 1)), tick=(dc == NDC - 1))
        w("act", "s_pe", m)
        ve = act(lambda E, j=j: E.copy(k_sb[:, j * CH:(j + 1) * CH], stA[:, 0:CH]))
        w("pe", "s_act", ve)
        for tt in range(KPC):
            tgt = stA[:, CH:CH + 128] if tt % 2 == 0 else stB[:, 0:128]
            for dc in range(NDC):
                m = pe(lambda E, tt=tt, dc=dc, s=slot, t=tgt: E.matmul(
                    t, lg_sb[s][:, dc, tt * 128:(tt + 1) * 128], wv_sb[:, dc, :],
                    start=(dc == 0), stop=(dc == NDC - 1)), tick=(dc == NDC - 1))
            w("act", "s_pe", m)
            ve = act(lambda E, j=j, tt=tt, t=tgt: E.copy(v_sb[:, j * KPC + tt, :], t))
            w("pe", "s_act", ve)
        lg_gate[slot] = cur("s_pe")

    q_totals = (cur("s_qe0"), cur("s_qe1"))
    fence()
    flush()
    p1c.close()

    p2 = ExitStack()
    q_t = [p2.enter_context(nc.sbuf_tensor(f"q_t{i}", [128, 2 * CH], BF16)) for i in range(2)]
    p_sb = [p2.enter_context(nc.sbuf_tensor(f"p_sb{i}", [128, 2 * CH], BF16)) for i in range(2)]
    o_t = [p2.enter_context(nc.sbuf_tensor(f"o_t{i}", [128, CH], F32)) for i in range(2)]
    o_b = [p2.enter_context(nc.sbuf_tensor(f"o_b{i}", [128, CH], BF16)) for i in range(2)]

    w("sync", "s_qe0", q_totals[0])
    w("sync", "s_qe1", q_totals[1])

    stX = [stA, stB]
    atst = {"o_ready": [0, 0], "sum_free": [0, 0], "ot_free": [0, 0],
            "ob_free": [0, 0], "st_gate": [0, 0], "fh": [None, None]}

    def finish_pre(half, mpv):
        Ob = pA if half == 0 else pB
        Sb = pC if half == 0 else pD
        w("act", "s_pe", mpv)
        v1 = act(lambda E: E.copy(r_srow[half][:, :], Sb[0:1, :]))
        w("act", "s_dve", atst["ot_free"][half])
        v2 = act(lambda E: E.copy(o_t[half][:, :], Ob[:, :]))
        atst["o_ready"][half] = v2
        w("dve", "s_act", v1)
        v3 = dve(lambda E: E.reciprocal(r_rinv[half][:, :], r_srow[half][:, :]))
        w("dve", "s_dve", v3)
        v4 = dve(lambda E: E.tensor_copy(r_rinvb[half][:, :], r_rinv[half][:, :]))
        atst["fh"][half] = (v1, v2, v4)

    def finish_post(half, hh, jq):
        Sb = pC if half == 0 else pD
        v1, v2, v4 = atst["fh"][half]
        w("pe", "s_dve", v4)
        w("pe", "s_act", v1)
        v5 = pe(lambda E: E.matmul(Sb[:, :], onesr_b[:, :], r_rinvb[half][:, :], start=True, stop=True))
        w("dve", "s_pe", v5)
        w("dve", "s_act", v2)
        w("dve", "s_ob0" if half == 0 else "s_ob1", atst["ob_free"][half])
        v6 = dve(lambda E: E.tensor_tensor(o_b[half][:, :], o_t[half][:, :], Sb[:, :], ALU.mult))
        atst["sum_free"][half] = v6
        atst["ot_free"][half] = v6
        w("gp", "s_dve", v6)
        dest = jq if jq < 8 else 15 - jq
        coff = 0 if jq < 8 else CH
        atst["ob_free"][half] = gp_store(
            a2a_in.ap()[512 * dest + 128 * hh: 512 * dest + 128 * (hh + 1), coff:coff + CH],
            o_b[half][:, :], "s_ob0" if half == 0 else "s_ob1")

    q_gate = [0, 0]
    for hh in range(4):
        for i in range(8):
            qslot = (hh * 8 + i) % 2
            qsem = "s_qa" if qslot == 0 else "s_qb"
            w("sync", "s_pe", q_gate[qslot])
            vq = sy_load(q_t[qslot][:, :],
                         q_spill.rearrange("(h p) t -> p h t", p=128)[:, hh, 2 * i * CH:(2 * i + 2) * CH],
                         qsem)
            w("pe", qsem, vq)
            first = [True, True]
            sfirst = [True, True]
            its = [(kc, kt) for kc in range(2 * i + 2) for kt in range(KPC)]
            nits = len(its)
            exp_mark = [0] * nits
            gate_mark = [None] * nits

            def emit_st(idx):
                kc, kt = its[idx]
                b = idx % 2
                both = kc <= 2 * i
                kg = kc * KPC + kt
                w("pe", "s_act", atst["st_gate"][b])
                if both:
                    pe(lambda E, b=b, kg=kg, qs=qslot: E.matmul(
                        stX[b][:, 0:CH], k_sb[:, kg * 128:(kg + 1) * 128], q_t[qs][:, 0:CH],
                        start=True, stop=True), tick=False)
                mst = pe(lambda E, b=b, kg=kg, qs=qslot: E.matmul(
                    stX[b][:, CH:2 * CH], k_sb[:, kg * 128:(kg + 1) * 128], q_t[qs][:, CH:2 * CH],
                    start=True, stop=True))
                w("act", "s_pe", mst)
                if both:
                    vex = act(lambda E, b=b: E.activation(p_sb[b][:, :], stX[b][:, :], AF.Exp, scale=ISQ))
                else:
                    vex = act(lambda E, b=b: E.activation(p_sb[b][:, CH:2 * CH], stX[b][:, CH:2 * CH],
                                                          AF.Exp, scale=ISQ))
                atst["st_gate"][b] = vex
                exp_mark[idx] = vex
                gate = ("s_act", vex)
                if kc == 2 * i and both:
                    w("dve", "s_act", vex)
                    vm = dve(lambda E, b=b, kt=kt: E.tensor_tensor(
                        p_sb[b][:, 0:CH], p_sb[b][:, 0:CH], masks_b[:, kt, :], ALU.mult))
                    gate = ("s_dve", vm)
                if kc == 2 * i + 1:
                    w("dve", "s_act", vex)
                    vm = dve(lambda E, b=b, kt=kt: E.tensor_tensor(
                        p_sb[b][:, CH:2 * CH], p_sb[b][:, CH:2 * CH], masks_b[:, kt, :], ALU.mult))
                    gate = ("s_dve", vm)
                gate_mark[idx] = gate

            def emit_sumpv(idx):
                kc, kt = its[idx]
                b = idx % 2
                both = kc <= 2 * i
                kg = kc * KPC + kt
                diag0 = (kc == 2 * i)
                diag1 = (kc == 2 * i + 1)
                stop0 = diag0 and kt == KPC - 1
                stop1 = diag1 and kt == KPC - 1
                g = gate_mark[idx]
                w("pe", g[0], g[1])
                mpv0 = None
                if both:
                    if first[0]:
                        w("pe", "s_act", atst["o_ready"][0])
                    if sfirst[0]:
                        w("pe", "s_dve", atst["sum_free"][0])
                    pe(lambda E, b=b, s0=sfirst[0], z0=stop0: E.matmul(
                        pC[0:1, :], ones_b[:, :], p_sb[b][:, 0:CH], start=s0, stop=z0), tick=False)
                    mpv0 = pe(lambda E, b=b, kg=kg, f0=first[0], z0=stop0: E.matmul(
                        pA[:, :], v_sb[:, kg, :], p_sb[b][:, 0:CH], start=f0, stop=z0))
                    first[0] = False
                    sfirst[0] = False
                if first[1]:
                    w("pe", "s_act", atst["o_ready"][1])
                if sfirst[1]:
                    w("pe", "s_dve", atst["sum_free"][1])
                pe(lambda E, b=b, s1=sfirst[1], z1=stop1: E.matmul(
                    pD[0:1, :], ones_b[:, :], p_sb[b][:, CH:2 * CH], start=s1, stop=z1), tick=False)
                mpv1 = pe(lambda E, b=b, kg=kg, f1=first[1], z1=stop1: E.matmul(
                    pB[:, :], v_sb[:, kg, :], p_sb[b][:, CH:2 * CH], start=f1, stop=z1))
                first[1] = False
                sfirst[1] = False
                return mpv0, mpv1

            emit_st(0)
            for idx in range(nits):
                kc, kt = its[idx]
                if idx + 1 < nits:
                    emit_st(idx + 1)
                mpv0, mpv1 = emit_sumpv(idx)
                fin0 = (kc == 2 * i and kt == KPC - 1)
                fin1 = (kc == 2 * i + 1 and kt == KPC - 1)
                if fin0:
                    finish_pre(0, mpv0)
                    finish_post(0, hh, 2 * i)
                if fin1:
                    finish_pre(1, mpv1)
                    finish_post(1, hh, 2 * i + 1)
            q_gate[qslot] = cur("s_pe")

    w("gp", "s_ob0", cur("s_ob0"))
    w("gp", "s_ob1", cur("s_ob1"))
    inc("s_cc")

    def _a2a(E):
        with allow_ici_all_to_all():
            return E.collective_compute(
                "AllToAll", ALU.bypass, replica_groups=[list(range(NCORES))],
                ins=[a2a_in.ap()], outs=[a2a_out.ap()])
    ops["gp"].append(lambda E: _a2a(E).then_inc(SEM["s_cc"]))

    fence(("pe", "act", "dve", "sync"))
    flush()
    p2.close()
    pkv.close()
    o_gat = a2a_out.ap().rearrange("(c p) t -> p c t", p=128)
    w("sync", "s_cc", 2)
    x1_r = x1_d.rearrange("(c p) t -> p c t", p=128)
    dn_r = dn_d.ap().rearrange("(c p) t -> p c t", p=128)
    out_r = out_fm.rearrange("(c p) t -> p c t", p=128)

    for qc in range(NCHK):
        cols = slice(qc * CH, (qc + 1) * CH)
        pq = ExitStack()
        la_g = pq.enter_context(nc.sbuf_tensor(f"la_g_{qc}", [128, CH], BF16))
        la_u = pq.enter_context(nc.sbuf_tensor(f"la_u_{qc}", [128, CH], BF16))
        lad_b = pq.enter_context(nc.sbuf_tensor(f"lad_b_{qc}", [128, CH], BF16))
        vz1 = dve(lambda E: E.memset(la_g[:, :], 0))
        vz2 = dve(lambda E: E.memset(la_u[:, :], 0))
        vz3 = dve(lambda E: E.memset(lad_b[:, :], 0))
        ln3b = pq.enter_context(nc.sbuf_tensor(f"ln3b_{qc}", [128, NDC, CH], BF16))

        p3 = ExitStack()
        o_sb = p3.enter_context(nc.sbuf_tensor(f"o_sb_{qc}", [128, NDC, CH], BF16))
        wo_sb = [p3.enter_context(nc.sbuf_tensor(f"wo_sb{i}_{qc}", [128, NDC, 128], BF16)) for i in range(2)]
        ap_sb = p3.enter_context(nc.sbuf_tensor(f"ap_sb_{qc}", [128, NDC, CH], BF16))
        sqb_t = p3.enter_context(nc.sbuf_tensor(f"sqb_t_{qc}", [128, CH], BF16))
        ab_t = p3.enter_context(nc.sbuf_tensor(f"ab_t_{qc}", [128, CH], BF16))
        x_t = [p3.enter_context(nc.sbuf_tensor(f"x_t{i}_{qc}", [128, CH], F32)) for i in range(2)]
        x1_t = [p3.enter_context(nc.sbuf_tensor(f"x1w_t{i}_{qc}", [128, CH], F32)) for i in range(2)]
        ap_t = p3.enter_context(nc.sbuf_tensor(f"ap_t_{qc}", [128, CH], F32))

        vo = sy_load(o_sb[:, :, :], o_gat[:, :, cols], "s_x")
        w("pe", "s_x", vo)

        wo_gate = [0, 0]
        sq_gate = 0
        last_stat = 0
        for dm in range(NDC):
            slot = dm % 2
            sname = "s_woa" if slot == 0 else "s_wob"
            w("gp", "s_pe", wo_gate[slot])
            vw = gp_load(wo_sb[slot][:, :, :], wo_t[dm], sname)
            w("pe", sname, vw)
            bank = pA if slot == 0 else pB
            for f in range(NDC):
                mq = pe(lambda E, f=f, s=slot, bk=bank: E.matmul(
                    bk[:, :], wo_sb[s][:, f, :], o_sb[:, f, :],
                    start=(f == 0), stop=(f == NDC - 1)), tick=(f == NDC - 1))
            wo_gate[slot] = mq
            w("act", "s_pe", mq)
            va1 = act(lambda E, dm=dm, bk=bank: E.copy(ap_sb[:, dm, :], bk[:, :]))
            w("pe", "s_act", va1)
            w("dve", "s_act", va1)
            w("dve", "s_pe", sq_gate)
            vsq = dve(lambda E, dm=dm: E.tensor_mul(sqb_t[:, :], ap_sb[:, dm, :], ap_sb[:, dm, :]))
            w("pe", "s_dve", vsq)
            pe(lambda E, dm=dm: E.matmul(pC[0:1, :], ones_b[:, :], ap_sb[:, dm, :],
                                         start=(dm == 0), stop=(dm == NDC - 1)), tick=False)
            last_stat = pe(lambda E, dm=dm: E.matmul(pD[0:1, :], ones_b[:, :], sqb_t[:, :],
                                                     start=(dm == 0), stop=(dm == NDC - 1)))
            sq_gate = last_stat

        bdone, stat_free = ln_rows(pC, pD, last_stat)
        w("dve", "s_pe", bdone)
        xt_gate = [0, 0]
        x1s_mark = [0] * NDC
        va2 = 0
        ln3_gate = 0
        for dm in range(NDC):
            slot = dm % 2
            sname = "s_xta" if slot == 0 else "s_xtb"
            ssem = "s_1sa" if slot == 0 else "s_1sb"
            w("sync", "s_dve", xt_gate[slot])
            vx = sy_load(x_t[slot][:, :], x_r[:, dm, cols], sname)
            if dm > 1:
                w("dve", ssem, x1s_mark[dm - 2])
            if dm > 0:
                w("dve", "s_dve", va2)
            t1 = dve(lambda E, dm=dm: E.scalar_tensor_tensor(
                ap_t[:, :], ap_sb[:, dm, :], 1.0, mean_t, ALU.mult, ALU.subtract))
            w("dve", "s_dve", t1)
            t2 = dve(lambda E: E.tensor_tensor(ap_t[:, :], ap_t[:, :], rstd_t, ALU.mult))
            w("dve", sname, vx)
            w("dve", "s_dve", t2)
            va2 = dve(lambda E, dm=dm, s=slot: E.tensor_tensor(
                x1_t[s][:, :], ap_t[:, :], x_t[s][:, :], ALU.add))
            xt_gate[slot] = va2
            w("dve", "s_dve", va2)
            w("dve", "s_pe", ln3_gate)
            vc1 = dve(lambda E, s=slot: E.tensor_copy(ab_t[:, :], x1_t[s][:, :]))
            w("dve", "s_dve", vc1)
            vc2 = dve(lambda E: E.tensor_mul(sqb_t[:, :], ab_t[:, :], ab_t[:, :]))
            w("pe", "s_dve", vc2)
            if dm == 0:
                w("pe", "s_act", stat_free)
            pe(lambda E, dm=dm: E.matmul(pC[0:1, :], ones_b[:, :], ab_t[:, :],
                                         start=(dm == 0), stop=(dm == NDC - 1)), tick=False)
            last_stat = pe(lambda E, dm=dm: E.matmul(pD[0:1, :], ones_b[:, :], sqb_t[:, :],
                                                     start=(dm == 0), stop=(dm == NDC - 1)))
            ln3_gate = last_stat
            w("gp", "s_dve", va2)
            x1s_mark[dm] = gp_store(x1_r[:, dm, cols], x1_t[slot][:, :], ssem)

        bdone, stat_free = ln_rows(pC, pD, last_stat, stA_gate=va2)
        w("dve", "s_pe", bdone)
        x1l_gate = list(xt_gate)
        vb3 = 0
        for dm in range(NDC):
            slot = dm % 2
            lsem = "s_x1ta" if slot == 0 else "s_x1tb"
            ssem = "s_1sa" if slot == 0 else "s_1sb"
            w("sync", ssem, x1s_mark[dm])
            w("sync", "s_dve", x1l_gate[slot])
            vxl = sy_load(x_t[slot][:, :], x1_r[:, dm, cols], lsem)
            w("dve", lsem, vxl)
            t1 = dve(lambda E, dm=dm, s=slot: E.scalar_tensor_tensor(
                ln3b[:, dm, :], x_t[s][:, :], 1.0, mean_t, ALU.mult, ALU.subtract))
            w("dve", "s_dve", t1)
            vb3 = dve(lambda E, dm=dm: E.tensor_tensor(ln3b[:, dm, :], ln3b[:, dm, :], rstd_t, ALU.mult))
            x1l_gate[slot] = vb3
        w("pe", "s_dve", vb3)

        mla = 0
        for dc in range(NDC):
            if dc == 0:
                w("pe", "s_act", stat_free)
            pe(lambda E, dc=dc: E.matmul(pC[0:RANK, :], ga_sb[:, dc, :], ln3b[:, dc, :],
                                         start=(dc == 0), stop=(dc == NDC - 1)), tick=False)
            mla = pe(lambda E, dc=dc: E.matmul(pD[0:RANK, :], ua_sb[:, dc, :], ln3b[:, dc, :],
                                               start=(dc == 0), stop=(dc == NDC - 1)))
        w("act", "s_pe", mla)
        w("act", "s_dve", vz2)
        act(lambda E: E.copy(la_g[0:RANK, :], pC[0:RANK, :]))
        vlu = act(lambda E: E.copy(la_u[0:RANK, :], pD[0:RANK, :]))
        w("pe", "s_act", vlu)

        fence()
        flush()
        p3.close()

        ph = ExitStack()
        h_sb = ph.enter_context(nc.sbuf_tensor(f"h_sb_{qc}", [128, NHC, CH], BF16))
        p4a = ExitStack()
        gw_sb = [p4a.enter_context(nc.sbuf_tensor(f"gw_sb{i}_{qc}", [128, NDC + 1, 128], BF16)) for i in range(2)]
        uw_sb = [p4a.enter_context(nc.sbuf_tensor(f"uw_sb{i}_{qc}", [128, NDC + 1, 128], BF16)) for i in range(2)]
        s_t = p4a.enter_context(nc.sbuf_tensor(f"s_t_{qc}", [128, CH], BF16))
        u_t = p4a.enter_context(nc.sbuf_tensor(f"u_t_{qc}", [128, CH], BF16))

        gu_gate = [0, 0]
        bank_gate = [0, 0]
        sil_gate = 0
        vh = 0
        mld = 0
        for hc in range(NHC):
            slot = hc % 2
            sname = "s_gua" if slot == 0 else "s_gub"
            w("gp", "s_pe", gu_gate[slot])
            gp_load(gw_sb[slot][:, :, :], gw_t[hc], sname)
            vwu = gp_load(uw_sb[slot][:, :, :], uw_t[hc], sname)
            w("pe", sname, vwu)
            bank = stX[slot]
            w("pe", "s_act", bank_gate[slot])
            for dc in range(NDC):
                pe(lambda E, dc=dc, s=slot, bk=bank: E.matmul(
                    bk[:, 0:CH], gw_sb[s][:, dc, :], ln3b[:, dc, :],
                    start=(dc == 0), stop=False), tick=False)
            pe(lambda E, s=slot, bk=bank: E.matmul(
                bk[:, 0:CH], gw_sb[s][:, NDC, :], la_g[:, :],
                start=False, stop=True), tick=False)
            for dc in range(NDC):
                pe(lambda E, dc=dc, s=slot, bk=bank: E.matmul(
                    bk[:, CH:2 * CH], uw_sb[s][:, dc, :], ln3b[:, dc, :],
                    start=(dc == 0), stop=False), tick=False)
            mu = pe(lambda E, s=slot, bk=bank: E.matmul(
                bk[:, CH:2 * CH], uw_sb[s][:, NDC, :], la_u[:, :],
                start=False, stop=True))
            gu_gate[slot] = mu
            w("act", "s_pe", mu)
            w("act", "s_dve", sil_gate)
            act(lambda E, bk=bank: E.activation(s_t[:, :], bk[:, 0:CH], AF.Silu))
            vu_ = act(lambda E, bk=bank: E.copy(u_t[:, :], bk[:, CH:2 * CH]))
            bank_gate[slot] = vu_
            w("dve", "s_act", vu_)
            vh = dve(lambda E, hc=hc: E.tensor_tensor(h_sb[:, hc, :], s_t[:, :], u_t[:, :], ALU.mult))
            sil_gate = vh
            w("pe", "s_dve", vh)
            if hc == 0:
                w("pe", "s_act", vlu)
            mld = pe(lambda E, hc=hc: E.matmul(pA[0:RANK, :], da_sb[:, hc, :], h_sb[:, hc, :],
                                               start=(hc == 0), stop=(hc == NHC - 1)))
        w("act", "s_pe", mld)
        w("act", "s_dve", vz3)
        vle = act(lambda E: E.copy(lad_b[0:RANK, :], pA[0:RANK, :]))
        w("pe", "s_act", vle)

        fence()
        flush()
        p4a.close()

        p4b = ExitStack()
        dwh = [p4b.enter_context(nc.sbuf_tensor(f"dwh{i}_{qc}", [128, 44, 128], BF16)) for i in range(2)]
        dn_t2 = [p4b.enter_context(nc.sbuf_tensor(f"dn_t2{i}_{qc}", [128, CH], BF16)) for i in range(2)]
        sqd_t = p4b.enter_context(nc.sbuf_tensor(f"sqd_t_{qc}", [128, CH], BF16))
        x1_t4 = [p4b.enter_context(nc.sbuf_tensor(f"x1r_t{i}_{qc}", [128, CH], F32)) for i in range(2)]
        out_t = [p4b.enter_context(nc.sbuf_tensor(f"out_t{i}_{qc}", [128, CH], F32)) for i in range(2)]
        dnl_t = [p4b.enter_context(nc.sbuf_tensor(f"dnl_t{i}_{qc}", [128, CH], BF16)) for i in range(2)]
        dnf_t = p4b.enter_context(nc.sbuf_tensor(f"dnf_t_{qc}", [128, CH], F32))

        dw_gate = [0, 0]
        sq_gate = 0
        last_stat = 0
        dns_mark = [0] * NDC
        dnt_gate = [0, 0]
        HALF0, HALF1 = 44, 43
        for dm in range(NDC):
            bank = pB if dm % 2 == 0 else pC
            mdn = 0
            for half in range(2):
                sname = "s_dwa" if half == 0 else "s_dwb"
                nch = HALF0 if half == 0 else HALF1
                base = 0 if half == 0 else HALF0
                w("gp", "s_pe", dw_gate[half])
                vw = gp_load(dwh[half][:, 0:nch, :], dw_t[dm][:, base:base + nch, :], sname)
                w("pe", sname, vw)
                for lc in range(nch):
                    hc = base + lc
                    rhs_h = (lambda E, hc=hc: h_sb[:, hc, :]) if hc < NHC else (lambda E: lad_b[:, :])
                    mdn = pe(lambda E, lc=lc, half=half, bk=bank, rh=rhs_h, last=(hc == NHC): E.matmul(
                        bk[:, :], dwh[half][:, lc, :], rh(E),
                        start=(half == 0 and lc == 0), stop=last), tick=(lc == nch - 1))
                dw_gate[half] = mdn
            slot = dm % 2
            dsem = "s_dsa" if slot == 0 else "s_dsb"
            w("act", "s_pe", mdn)
            if dm > 1:
                w("act", dsem, dns_mark[dm - 2])
            w("act", "s_dve", dnt_gate[slot])
            va1 = act(lambda E, bk=bank, s=slot: E.copy(dn_t2[s][:, :], bk[:, :]))
            w("pe", "s_act", va1)
            w("dve", "s_act", va1)
            w("dve", "s_pe", sq_gate)
            vsq = dve(lambda E, s=slot: E.tensor_mul(sqd_t[:, :], dn_t2[s][:, :], dn_t2[s][:, :]))
            dnt_gate[slot] = vsq
            w("pe", "s_dve", vsq)
            pe(lambda E, dm=dm, s=slot: E.matmul(pD[0:1, :], ones_b[:, :], dn_t2[s][:, :],
                                                 start=(dm == 0), stop=(dm == NDC - 1)), tick=False)
            last_stat = pe(lambda E, dm=dm: E.matmul(pA[0:1, :], ones_b[:, :], sqd_t[:, :],
                                                     start=(dm == 0), stop=(dm == NDC - 1)))
            sq_gate = last_stat
            w("gp", "s_act", va1)
            dns_mark[dm] = gp_store(dn_r[:, dm, cols], dn_t2[slot][:, :], dsem)

        bdone, stat_free = ln_rows(pD, pA, last_stat)
        w("dve", "s_pe", bdone)
        x1t_gate = [0, 0]
        ot_gate = [0, 0]
        dnl_gate = [0, 0]
        vo3 = 0
        for dm in range(NDC):
            slot = dm % 2
            lsem = "s_x1ta" if slot == 0 else "s_x1tb"
            dlsem = "s_dla" if slot == 0 else "s_dlb"
            dsem = "s_dsa" if slot == 0 else "s_dsb"
            w("sync", "s_dve", x1t_gate[slot])
            vx1 = sy_load(x1_t4[slot][:, :], x1_r[:, dm, cols], lsem)
            w("sync", dsem, dns_mark[dm])
            w("sync", "s_dve", dnl_gate[slot])
            vdl = sy_load(dnl_t[slot][:, :], dn_r[:, dm, cols], dlsem)
            if dm > 0:
                w("dve", "s_dve", vo3)
            w("dve", dlsem, vdl)
            t1 = dve(lambda E, s=slot: E.scalar_tensor_tensor(
                dnf_t[:, :], dnl_t[s][:, :], 1.0, mean_t, ALU.mult, ALU.subtract))
            dnl_gate[slot] = t1
            w("dve", "s_dve", t1)
            t2 = dve(lambda E: E.tensor_tensor(dnf_t[:, :], dnf_t[:, :], rstd_t, ALU.mult))
            w("dve", lsem, vx1)
            w("dve", "s_dve", t2)
            w("dve", "s_ot0" if slot == 0 else "s_ot1", ot_gate[slot])
            vo3 = dve(lambda E, s=slot: E.tensor_tensor(
                out_t[s][:, :], dnf_t[:, :], x1_t4[s][:, :], ALU.add))
            x1t_gate[slot] = vo3
            w("sync", "s_dve", vo3)
            ot_gate[slot] = sy_store(out_r[:, dm, cols], out_t[slot][:, :], "s_ot0" if slot == 0 else "s_ot1")
        w("pe", "s_dve", vo3)

        fence()
        flush()
        p4b.close()
        ph.close()
        pq.close()

    w("sync", "s_ot0", cur("s_ot0"))
    w("sync", "s_ot1", cur("s_ot1"))
    w("gp", "s_gst", cur("s_gst"))
    flush()
    st.close()
    return nc


def _stage(inputs):
    S = int(np.asarray(inputs["L0_attn_hidden_states"]).shape[1])
    CH = S // 16
    TPC = S // 8
    KPC = CH // 128
    hid = np.ascontiguousarray(np.asarray(inputs["L0_attn_hidden_states"], dtype=np.float32)[0])
    wq = np.asarray(inputs["L0_attn_w_q"], dtype=np.float32)
    wk = np.asarray(inputs["L0_attn_w_k"], dtype=np.float32)
    wv = np.asarray(inputs["L0_attn_w_v"], dtype=np.float32)
    wo = np.asarray(inputs["L0_attn_w_o"], dtype=np.float32)
    gw = np.asarray(inputs["L0_ffn_gate_w"], dtype=np.float32)
    uw = np.asarray(inputs["L0_ffn_up_w"], dtype=np.float32)
    dw = np.asarray(inputs["L0_ffn_down_w"], dtype=np.float32)
    ga = np.asarray(inputs["L0_ffn_gate_a"], dtype=np.float32)
    gb = np.asarray(inputs["L0_ffn_gate_b"], dtype=np.float32)
    ua = np.asarray(inputs["L0_ffn_up_a"], dtype=np.float32)
    ub = np.asarray(inputs["L0_ffn_up_b"], dtype=np.float32)
    da = np.asarray(inputs["L0_ffn_down_a"], dtype=np.float32)
    db = np.asarray(inputs["L0_ffn_down_b"], dtype=np.float32)

    def pmaj(strips):
        n, K, _ = strips.shape
        return np.ascontiguousarray(strips.reshape(n, K // 128, 128, 128).transpose(0, 2, 1, 3))

    def pad_strips(wT, b, nstr):
        K = wT.shape[0]
        out = np.zeros((nstr, K + 128, 128), np.float32)
        for i in range(nstr):
            out[i, :K, :] = wT[:, i * 128:(i + 1) * 128]
            out[i, K:K + RANK, :] = b[i * 128:(i + 1) * 128, :].T
        return pmaj(out)

    shared = {
        "wo_t": pmaj(np.ascontiguousarray(wo.reshape(DM, NDC, 128).transpose(1, 0, 2))),
        "gw_t": pad_strips(gw.T, gb, NHC),
        "uw_t": pad_strips(uw.T, ub, NHC),
        "dw_t": pad_strips(dw.T, db, NDC),
        "ga_t": np.ascontiguousarray(ga.T),
        "ua_t": np.ascontiguousarray(ua.T),
        "da_t": np.ascontiguousarray(da.T),
        "ones_i": np.ones((128, 1), np.float32),
        "onesr_i": np.ones((1, 128), np.float32),
    }
    k_idx = np.arange(128)[:, None]
    q_idx = np.arange(CH)[None, :]
    masks = np.empty((128, KPC, CH), np.float32)
    for kt in range(KPC):
        masks[:, kt, :] = ((kt * 128 + k_idx) <= q_idx).astype(np.float32)
    shared["masks_i"] = masks

    in_maps = []
    for c in range(NCORES):
        toks = np.concatenate([np.arange(c * CH, (c + 1) * CH),
                               np.arange((15 - c) * CH, (16 - c) * CH)])
        m = dict(shared)
        m["x_fm"] = np.ascontiguousarray(hid[toks].T)
        m["wq_t"] = pmaj(np.ascontiguousarray(
            wq[:, 512 * c:512 * (c + 1)].reshape(DM, 4, 128).transpose(1, 0, 2)))
        m["wk_t"] = pmaj(wk[:, 128 * c:128 * (c + 1)][None])[0]
        m["wv_t"] = pmaj(wv[:, 128 * c:128 * (c + 1)][None])[0]
        in_maps.append(m)
    return S, CH, in_maps


def kernel(**inputs):
    S, CH, in_maps = _stage(inputs)
    nc = build(S)
    res = run_bass_kernel_spmd(nc, in_maps, core_ids=list(range(NCORES)))
    out = np.empty((1, S, DM), np.float32)
    for c in range(NCORES):
        of = res.results[c]["out_fm"]
        out[0, c * CH:(c + 1) * CH] = of[:, :CH].T
        out[0, (15 - c) * CH:(16 - c) * CH] = of[:, CH:].T
    return out


# version 06
# speedup vs baseline: 1.1996x; 1.1124x over previous
import math
import numpy as np
from contextlib import ExitStack

import concourse.bass as bass
import concourse.mybir as mybir
from concourse.bass_utils import run_bass_kernel_spmd
from concourse.aot_env import allow_ici_all_to_all

F32 = mybir.dt.float32
BF16 = mybir.dt.bfloat16
AF = mybir.ActivationFunctionType
ALU = mybir.AluOpType

NCORES = 8
NH, HKV, G, HD = 32, 8, 4, 128
DM, DH, RANK = 4096, 11008, 16
NDC = DM // 128
NHC = DH // 128
NHH = DH // 2
EPS = 1e-5
ISQ = 1.0 / math.sqrt(HD)
ENGS = ("pe", "act", "dve", "sync", "gp")
SEC = {"pe": "tensor", "act": "scalar", "dve": "vector", "sync": "sync", "gp": "gpsimd"}

SEM_NAMES = ["s_gw", "s_cc", "s_x", "s_of", "s_lga", "s_lgb", "s_qa", "s_qb",
             "s_woa", "s_wob", "s_gua", "s_gub", "s_dwa", "s_dwb",
             "s_xta", "s_xtb", "s_x1ta", "s_x1tb", "s_st", "s_gst",
             "s_qe0", "s_qe1", "s_ob0", "s_ob1", "s_ot0", "s_ot1",
             "s_1sa", "s_1sb", "s_dsa", "s_dsb", "s_dla", "s_dlb",
             "s_pe", "s_act", "s_dve"]


def build(S):
    CH = S // 16
    TPC = S // 8
    KPC = CH // 128
    NKT = S // 128
    assert CH % 128 == 0 and 128 <= CH <= 512

    nc = bass.Bass("TRN2", target_bir_lowering=False, debug=False)

    def din(n, sh):
        return nc.dram_tensor(n, sh, F32, kind="ExternalInput").ap()

    x_fm = din("x_fm", [DM, TPC])
    wq_t = din("wq_t", [4, 128, NDC, 128])
    wk_t = din("wk_t", [128, NDC, 128])
    wv_t = din("wv_t", [128, NDC, 128])
    wo_t = din("wo_t", [NDC, 128, NDC, 128])
    gw_t = din("gw_t", [NHC, 128, NDC + 1, 128])
    uw_t = din("uw_t", [NHC, 128, NDC + 1, 128])
    dw_t = din("dw_t", [NDC, 128, 87, 128])
    ga_t = din("ga_t", [DM, RANK])
    ua_t = din("ua_t", [DM, RANK])
    da_t = din("da_t", [DH, RANK])
    masks_i = din("masks_i", [128, KPC, CH])
    ones_i = din("ones_i", [128, 1])
    onesr_i = din("onesr_i", [1, 128])

    out_fm = nc.dram_tensor("out_fm", [DM, TPC], F32, kind="ExternalOutput").ap()

    ln_shard = nc.dram_tensor("ln_shard", [DM, TPC], BF16)
    ln_gat = nc.dram_tensor("ln_gat", [NCORES * DM, TPC], BF16)
    q_spill = nc.dram_tensor("q_spill", [512, S], BF16).ap()
    a2a_in0 = nc.dram_tensor("a2a_in0", [NCORES * 512, CH], BF16)
    a2a_in1 = nc.dram_tensor("a2a_in1", [NCORES * 512, CH], BF16)
    a2a_out0 = nc.dram_tensor("a2a_out0", [NCORES * 512, CH], BF16)
    a2a_out1 = nc.dram_tensor("a2a_out1", [NCORES * 512, CH], BF16)
    x1_d = nc.dram_tensor("x1_d", [DM, TPC], F32).ap()
    dn_d = nc.dram_tensor("dn_d", [DM, TPC], BF16)

    ops = {e: [] for e in ENGS}
    cnt = {}
    SEM = {}

    def inc(sem, k=1):
        cnt[sem] = cnt.get(sem, 0) + k
        return cnt[sem]

    def cur(sem):
        return cnt.get(sem, 0)

    def w(eng, sem, val):
        if val and val > 0:
            ops[eng].append(lambda E, s=sem, v=val: E.wait_ge(SEM[s], v))

    def gp_load(dst, src, sem):
        v = inc(sem, 16)
        ops["gp"].append(lambda E, o=dst, i=src, s=sem: E.dma_start(out=o, in_=i).then_inc(SEM[s], 16))
        return v

    def sy_load(dst, src, sem):
        v = inc(sem, 16)
        ops["sync"].append(lambda E, o=dst, i=src, s=sem: E.dma_start(out=o, in_=i).then_inc(SEM[s], 16))
        return v

    def sy_store(dst, src, sem="s_st"):
        v = inc(sem, 16)
        ops["sync"].append(lambda E, o=dst, i=src, s=sem: E.dma_start(out=o, in_=i).then_inc(SEM[s], 16))
        return v

    def gp_store(dst, src, sem="s_gst"):
        v = inc(sem, 16)
        ops["gp"].append(lambda E, o=dst, i=src, s=sem: E.dma_start(out=o, in_=i).then_inc(SEM[s], 16))
        return v

    def pe(fn, tick=True):
        if tick:
            v = inc("s_pe")
            ops["pe"].append(lambda E, f=fn: f(E).then_inc(SEM["s_pe"], 1))
            return v
        ops["pe"].append(lambda E, f=fn: f(E))
        return None

    def act(fn):
        v = inc("s_act")
        ops["act"].append(lambda E, f=fn: f(E).then_inc(SEM["s_act"], 1))
        return v

    def dve(fn):
        v = inc("s_dve")
        ops["dve"].append(lambda E, f=fn: f(E).then_inc(SEM["s_dve"], 1))
        return v

    STORE_SEMS = ("s_gst", "s_st", "s_qe0", "s_qe1", "s_ob0", "s_ob1", "s_ot0", "s_ot1", "s_1sa", "s_1sb", "s_dsa", "s_dsb")

    def fence(engines=ENGS, sems=("s_pe", "s_act", "s_dve")):
        marks = {k: cur(k) for k in sems + STORE_SEMS}
        for e in engines:
            for sname, v in marks.items():
                if v:
                    w(e, sname, v)

    st = ExitStack()

    ones_b = st.enter_context(nc.sbuf_tensor("ones_b", [128, 1], BF16))
    onesr_f = st.enter_context(nc.sbuf_tensor("onesr_f", [1, 128], F32))
    onesr_b = st.enter_context(nc.sbuf_tensor("onesr_b", [1, 128], BF16))
    ga_sb = st.enter_context(nc.sbuf_tensor("ga_sb", [128, NDC, RANK], BF16))
    ua_sb = st.enter_context(nc.sbuf_tensor("ua_sb", [128, NDC, RANK], BF16))
    da_sb = st.enter_context(nc.sbuf_tensor("da_sb", [128, NHC, RANK], BF16))
    r_sum = st.enter_context(nc.sbuf_tensor("r_sum", [1, CH], F32))
    r_sq = st.enter_context(nc.sbuf_tensor("r_sq", [1, CH], F32))
    r_mean = st.enter_context(nc.sbuf_tensor("r_mean", [1, CH], F32))
    r_var = st.enter_context(nc.sbuf_tensor("r_var", [1, CH], F32))
    r_tmp = st.enter_context(nc.sbuf_tensor("r_tmp", [1, CH], F32))
    r_rstd = st.enter_context(nc.sbuf_tensor("r_rstd", [1, CH], F32))
    r_srow = [st.enter_context(nc.sbuf_tensor(f"r_srow{i}", [1, CH], F32)) for i in range(2)]
    r_rinv = [st.enter_context(nc.sbuf_tensor(f"r_rinv{i}", [1, CH], F32)) for i in range(2)]
    r_rinvb = [st.enter_context(nc.sbuf_tensor(f"r_rinvb{i}", [1, CH], BF16)) for i in range(2)]

    stA = st.enter_context(nc.psum_tensor("stA", [128, 2 * CH], F32))
    stB = st.enter_context(nc.psum_tensor("stB", [128, 2 * CH], F32))
    pA = st.enter_context(nc.psum_tensor("pA", [128, CH], F32))
    pB = st.enter_context(nc.psum_tensor("pB", [128, CH], F32))
    pC = st.enter_context(nc.psum_tensor("pC", [128, CH], F32))
    pD = st.enter_context(nc.psum_tensor("pD", [128, CH], F32))

    for n in SEM_NAMES:
        SEM[n] = st.enter_context(nc.semaphore(n))
    block = st.enter_context(nc.Block())

    def flush():
        for e in ENGS:
            if ops[e]:
                fns = ops[e]
                ops[e] = []

                def run(E, fns=fns):
                    for f in fns:
                        f(E)
                getattr(block, SEC[e])(run)

    mean_bc = bass.AP(stA, 0, [[2 * CH, 128], [0, NDC], [1, CH]])
    rstd_bc = bass.AP(stA, CH, [[2 * CH, 128], [0, NDC], [1, CH]])
    mean_t = bass.AP(stA, 0, [[2 * CH, 128], [1, CH]])
    rstd_t = bass.AP(stA, CH, [[2 * CH, 128], [1, CH]])

    def ln_rows(sum_bank, sq_bank, pe_gate, stA_gate=0):
        w("act", "s_pe", pe_gate)
        v1 = act(lambda E: E.copy(r_sum[:, :], sum_bank[0:1, 0:CH]))
        v2 = act(lambda E: E.copy(r_sq[:, :], sq_bank[0:1, 0:CH]))
        w("dve", "s_act", v2)
        a1 = dve(lambda E: E.tensor_scalar_mul(r_mean[:, :], r_sum[:, :], 1.0 / DM))
        w("dve", "s_dve", a1)
        a2 = dve(lambda E: E.tensor_scalar_mul(r_sq[:, :], r_sq[:, :], 1.0 / DM))
        w("dve", "s_dve", a2)
        a3 = dve(lambda E: E.tensor_mul(r_tmp[:, :], r_mean[:, :], r_mean[:, :]))
        w("dve", "s_dve", a3)
        a4 = dve(lambda E: E.tensor_sub(r_var[:, :], r_sq[:, :], r_tmp[:, :]))
        w("dve", "s_dve", a4)
        a4b = dve(lambda E: E.tensor_scalar_add(r_var[:, :], r_var[:, :], EPS))
        w("act", "s_dve", a4b)
        a5 = act(lambda E: E.activation(r_tmp[:, :], r_var[:, :], AF.Ln))
        w("act", "s_act", a5)
        a6 = act(lambda E: E.activation(r_rstd[:, :], r_tmp[:, :], AF.Exp, scale=-0.5))
        w("pe", "s_act", a6)
        if stA_gate:
            w("pe", "s_dve", stA_gate)
        pe(lambda E: E.matmul(stA[:, 0:CH], onesr_f[:, :], r_mean[:, :], start=True, stop=True), tick=False)
        b2 = pe(lambda E: E.matmul(stA[:, CH:2 * CH], onesr_f[:, :], r_rstd[:, :], start=True, stop=True))
        return b2, v2

    gp_load(ga_sb[:, :, :], ga_t.rearrange("(c p) r -> p c r", p=128), "s_gw")
    gp_load(ua_sb[:, :, :], ua_t.rearrange("(c p) r -> p c r", p=128), "s_gw")
    gp_load(da_sb[:, :, :], da_t.rearrange("(c p) r -> p c r", p=128), "s_gw")
    gp_load(ones_b[:, :], ones_i, "s_gw")
    v_ob = gp_load(onesr_b[:, :], onesr_i, "s_gw")
    v_of = sy_load(onesr_f[:, :], onesr_i, "s_of")

    p1 = ExitStack()
    x_sb = p1.enter_context(nc.sbuf_tensor("x_sb", [128, NDC, CH], F32))
    ln1b = p1.enter_context(nc.sbuf_tensor("ln1b", [128, NDC, CH], BF16))
    xb_t = p1.enter_context(nc.sbuf_tensor("xb_t", [128, CH], BF16))
    sq_t = p1.enter_context(nc.sbuf_tensor("sq_t", [128, CH], BF16))

    x_r = x_fm.rearrange("(c p) t -> p c t", p=128)
    NCHK = TPC // CH
    vb = 0
    stat_free = 0
    for cc in range(NCHK):
        vx = sy_load(x_sb[:, :, :], x_r[:, :, cc * CH:(cc + 1) * CH], "s_x")
        w("dve", "s_x", vx)
        m2 = 0
        for dc in range(NDC):
            if dc > 0:
                w("dve", "s_pe", m2)
            vc = dve(lambda E, dc=dc: E.tensor_copy(xb_t[:, :], x_sb[:, dc, :]))
            w("dve", "s_dve", vc)
            vs = dve(lambda E: E.tensor_mul(sq_t[:, :], xb_t[:, :], xb_t[:, :]))
            w("pe", "s_dve", vs)
            if cc == 0 and dc == 0:
                w("pe", "s_gw", v_ob)
                w("pe", "s_of", v_of)
            if dc == 0 and stat_free:
                w("pe", "s_act", stat_free)
            pe(lambda E, dc=dc: E.matmul(pC[0:1, :], ones_b[:, :], xb_t[:, :],
                                         start=(dc == 0), stop=(dc == NDC - 1)), tick=False)
            m2 = pe(lambda E, dc=dc: E.matmul(pD[0:1, :], ones_b[:, :], sq_t[:, :],
                                              start=(dc == 0), stop=(dc == NDC - 1)))
        bdone, stat_free = ln_rows(pC, pD, m2, stA_gate=vb)
        w("dve", "s_pe", bdone)
        if cc > 0:
            w("dve", "s_gst", cur("s_gst"))
        lsl = ln1b[:, :, :]
        va = dve(lambda E, l=lsl: E.scalar_tensor_tensor(
            l, x_sb[:, :, :], 1.0, mean_bc, ALU.mult, ALU.subtract))
        w("dve", "s_dve", va)
        vb = dve(lambda E, l=lsl: E.tensor_tensor(l, l, rstd_bc, ALU.mult))
        w("gp", "s_dve", vb)
        gp_store(ln_shard.ap().rearrange("(c p) t -> p c t", p=128)[:, :, cc * CH:(cc + 1) * CH],
                 ln1b[:, :, :])
        if cc + 1 < NCHK:
            w("sync", "s_dve", vb)

    w("gp", "s_gst", cur("s_gst"))
    inc("s_cc")
    ops["gp"].append(lambda E: E.collective_compute(
        "AllGather", ALU.bypass, replica_groups=[list(range(NCORES))],
        ins=[ln_shard.ap()], outs=[ln_gat.ap()]).then_inc(SEM["s_cc"]))

    fence(("pe", "act", "dve"))
    flush()
    p1.close()

    pkv = ExitStack()
    masks_b = pkv.enter_context(nc.sbuf_tensor("masks_b", [128, KPC, CH], BF16))
    k_sb = pkv.enter_context(nc.sbuf_tensor("k_sb", [128, S], BF16))
    v_sb = pkv.enter_context(nc.sbuf_tensor("v_sb", [128, NKT, 128], BF16))

    p1c = ExitStack()
    wq_sb = p1c.enter_context(nc.sbuf_tensor("wq_sb", [128, 4, NDC, 128], BF16))
    wk_sb = p1c.enter_context(nc.sbuf_tensor("wk_sb", [128, NDC, 128], BF16))
    wv_sb = p1c.enter_context(nc.sbuf_tensor("wv_sb", [128, NDC, 128], BF16))
    lg_sb = [p1c.enter_context(nc.sbuf_tensor(f"lg_sb{i}", [128, NDC, CH], BF16)) for i in range(2)]
    q_ev = [p1c.enter_context(nc.sbuf_tensor(f"q_ev{i}", [128, CH], BF16)) for i in range(2)]

    gp_load(masks_b[:, :, :], masks_i, "s_gw")
    for _hh in range(4):
        gp_load(wq_sb[:, _hh, :, :], wq_t[_hh], "s_gw")
    gp_load(wk_sb[:, :, :], wk_t, "s_gw")
    vwv = gp_load(wv_sb[:, :, :], wv_t, "s_gw")
    w("pe", "s_gw", vwv)
    w("sync", "s_cc", 1)

    def lg_src(j):
        r, off = (j, 0) if j < 8 else (15 - j, 1)
        return ln_gat.ap()[DM * r: DM * (r + 1), off * CH:(off + 1) * CH] \
            .rearrange("(c p) t -> p c t", p=128)

    qbanks = [pA, pB, pC, pD]
    lg_gate = [0, 0]
    qev_gate = [0, 0]
    for j in range(16):
        slot = j % 2
        sname = "s_lga" if slot == 0 else "s_lgb"
        w("sync", "s_pe", lg_gate[slot])
        vl = sy_load(lg_sb[slot][:, :, :], lg_src(j), sname)
        w("pe", sname, vl)
        for hh in range(4):
            for dc in range(NDC):
                m = pe(lambda E, hh=hh, dc=dc, s=slot: E.matmul(
                    qbanks[hh][:, :], wq_sb[:, hh, dc, :], lg_sb[s][:, dc, :],
                    start=(dc == 0), stop=(dc == NDC - 1)), tick=(dc == NDC - 1))
            w("act", "s_pe", m)
            w("act", "s_qe0" if hh % 2 == 0 else "s_qe1", qev_gate[hh % 2])
            ve = act(lambda E, hh=hh: E.copy(q_ev[hh % 2][:, :], qbanks[hh][:, :]))
            w("pe", "s_act", ve)
            w("gp", "s_act", ve)
            qev_gate[hh % 2] = gp_store(
                q_spill.rearrange("(h p) t -> p h t", p=128)[:, hh, j * CH:(j + 1) * CH],
                q_ev[hh % 2][:, :], "s_qe0" if hh % 2 == 0 else "s_qe1")
        for dc in range(NDC):
            m = pe(lambda E, dc=dc, s=slot: E.matmul(
                stA[:, 0:CH], wk_sb[:, dc, :], lg_sb[s][:, dc, :],
                start=(dc == 0), stop=(dc == NDC - 1)), tick=(dc == NDC - 1))
        w("act", "s_pe", m)
        ve = act(lambda E, j=j: E.copy(k_sb[:, j * CH:(j + 1) * CH], stA[:, 0:CH]))
        w("pe", "s_act", ve)
        for tt in range(KPC):
            tgt = stA[:, CH:CH + 128] if tt % 2 == 0 else stB[:, 0:128]
            for dc in range(NDC):
                m = pe(lambda E, tt=tt, dc=dc, s=slot, t=tgt: E.matmul(
                    t, lg_sb[s][:, dc, tt * 128:(tt + 1) * 128], wv_sb[:, dc, :],
                    start=(dc == 0), stop=(dc == NDC - 1)), tick=(dc == NDC - 1))
            w("act", "s_pe", m)
            ve = act(lambda E, j=j, tt=tt, t=tgt: E.copy(v_sb[:, j * KPC + tt, :], t))
            w("pe", "s_act", ve)
        lg_gate[slot] = cur("s_pe")

    q_totals = (cur("s_qe0"), cur("s_qe1"))
    fence()
    flush()
    p1c.close()

    p2 = ExitStack()
    q_t = [p2.enter_context(nc.sbuf_tensor(f"q_t{i}", [128, 2 * CH], BF16)) for i in range(2)]
    p_sb = [p2.enter_context(nc.sbuf_tensor(f"p_sb{i}", [128, 2 * CH], BF16)) for i in range(2)]
    o_t = [p2.enter_context(nc.sbuf_tensor(f"o_t{i}", [128, CH], F32)) for i in range(2)]
    o_b = [p2.enter_context(nc.sbuf_tensor(f"o_b{i}", [128, CH], BF16)) for i in range(2)]

    w("sync", "s_qe0", q_totals[0])
    w("sync", "s_qe1", q_totals[1])

    stX = [stA, stB]
    atst = {"o_ready": [0, 0], "sum_free": [0, 0], "ot_free": [0, 0],
            "ob_free": [0, 0], "st_gate": [0, 0], "fh": [None, None]}

    def finish_pre(half, mpv):
        Ob = pA if half == 0 else pB
        Sb = pC if half == 0 else pD
        w("act", "s_pe", mpv)
        v1 = act(lambda E: E.copy(r_srow[half][:, :], Sb[0:1, :]))
        w("act", "s_dve", atst["ot_free"][half])
        v2 = act(lambda E: E.copy(o_t[half][:, :], Ob[:, :]))
        atst["o_ready"][half] = v2
        w("dve", "s_act", v1)
        v3 = dve(lambda E: E.reciprocal(r_rinv[half][:, :], r_srow[half][:, :]))
        w("dve", "s_dve", v3)
        v4 = dve(lambda E: E.tensor_copy(r_rinvb[half][:, :], r_rinv[half][:, :]))
        atst["fh"][half] = (v1, v2, v4)

    def finish_post(half, hh, jq):
        Sb = pC if half == 0 else pD
        v1, v2, v4 = atst["fh"][half]
        w("pe", "s_dve", v4)
        w("pe", "s_act", v1)
        v5 = pe(lambda E: E.matmul(Sb[:, :], onesr_b[:, :], r_rinvb[half][:, :], start=True, stop=True))
        w("dve", "s_pe", v5)
        w("dve", "s_act", v2)
        w("dve", "s_ob0" if half == 0 else "s_ob1", atst["ob_free"][half])
        v6 = dve(lambda E: E.tensor_tensor(o_b[half][:, :], o_t[half][:, :], Sb[:, :], ALU.mult))
        atst["sum_free"][half] = v6
        atst["ot_free"][half] = v6
        w("gp", "s_dve", v6)
        dest = jq if jq < 8 else 15 - jq
        tgt = a2a_in0 if jq < 8 else a2a_in1
        atst["ob_free"][half] = gp_store(
            tgt.ap()[512 * dest + 128 * hh: 512 * dest + 128 * (hh + 1), :],
            o_b[half][:, :], "s_ob0" if half == 0 else "s_ob1")

    q_gate = [0, 0]
    for i in range(8):
        for hh in range(4):
            qslot = (i * 4 + hh) % 2
            qsem = "s_qa" if qslot == 0 else "s_qb"
            w("sync", "s_pe", q_gate[qslot])
            vq = sy_load(q_t[qslot][:, :],
                         q_spill.rearrange("(h p) t -> p h t", p=128)[:, hh, 2 * i * CH:(2 * i + 2) * CH],
                         qsem)
            w("pe", qsem, vq)
            first = [True, True]
            sfirst = [True, True]
            its = [(kc, kt) for kc in range(2 * i + 2) for kt in range(KPC)]
            nits = len(its)
            exp_mark = [0] * nits
            gate_mark = [None] * nits

            def emit_st(idx):
                kc, kt = its[idx]
                b = idx % 2
                both = kc <= 2 * i
                kg = kc * KPC + kt
                w("pe", "s_act", atst["st_gate"][b])
                if both:
                    pe(lambda E, b=b, kg=kg, qs=qslot: E.matmul(
                        stX[b][:, 0:CH], k_sb[:, kg * 128:(kg + 1) * 128], q_t[qs][:, 0:CH],
                        start=True, stop=True), tick=False)
                mst = pe(lambda E, b=b, kg=kg, qs=qslot: E.matmul(
                    stX[b][:, CH:2 * CH], k_sb[:, kg * 128:(kg + 1) * 128], q_t[qs][:, CH:2 * CH],
                    start=True, stop=True))
                w("act", "s_pe", mst)
                if both:
                    vex = act(lambda E, b=b: E.activation(p_sb[b][:, :], stX[b][:, :], AF.Exp, scale=ISQ))
                else:
                    vex = act(lambda E, b=b: E.activation(p_sb[b][:, CH:2 * CH], stX[b][:, CH:2 * CH],
                                                          AF.Exp, scale=ISQ))
                atst["st_gate"][b] = vex
                exp_mark[idx] = vex
                gate = ("s_act", vex)
                if kc == 2 * i and both:
                    w("dve", "s_act", vex)
                    vm = dve(lambda E, b=b, kt=kt: E.tensor_tensor(
                        p_sb[b][:, 0:CH], p_sb[b][:, 0:CH], masks_b[:, kt, :], ALU.mult))
                    gate = ("s_dve", vm)
                if kc == 2 * i + 1:
                    w("dve", "s_act", vex)
                    vm = dve(lambda E, b=b, kt=kt: E.tensor_tensor(
                        p_sb[b][:, CH:2 * CH], p_sb[b][:, CH:2 * CH], masks_b[:, kt, :], ALU.mult))
                    gate = ("s_dve", vm)
                gate_mark[idx] = gate

            def emit_sumpv(idx):
                kc, kt = its[idx]
                b = idx % 2
                both = kc <= 2 * i
                kg = kc * KPC + kt
                diag0 = (kc == 2 * i)
                diag1 = (kc == 2 * i + 1)
                stop0 = diag0 and kt == KPC - 1
                stop1 = diag1 and kt == KPC - 1
                g = gate_mark[idx]
                w("pe", g[0], g[1])
                mpv0 = None
                if both:
                    if first[0]:
                        w("pe", "s_act", atst["o_ready"][0])
                    if sfirst[0]:
                        w("pe", "s_dve", atst["sum_free"][0])
                    pe(lambda E, b=b, s0=sfirst[0], z0=stop0: E.matmul(
                        pC[0:1, :], ones_b[:, :], p_sb[b][:, 0:CH], start=s0, stop=z0), tick=False)
                    mpv0 = pe(lambda E, b=b, kg=kg, f0=first[0], z0=stop0: E.matmul(
                        pA[:, :], v_sb[:, kg, :], p_sb[b][:, 0:CH], start=f0, stop=z0))
                    first[0] = False
                    sfirst[0] = False
                if first[1]:
                    w("pe", "s_act", atst["o_ready"][1])
                if sfirst[1]:
                    w("pe", "s_dve", atst["sum_free"][1])
                pe(lambda E, b=b, s1=sfirst[1], z1=stop1: E.matmul(
                    pD[0:1, :], ones_b[:, :], p_sb[b][:, CH:2 * CH], start=s1, stop=z1), tick=False)
                mpv1 = pe(lambda E, b=b, kg=kg, f1=first[1], z1=stop1: E.matmul(
                    pB[:, :], v_sb[:, kg, :], p_sb[b][:, CH:2 * CH], start=f1, stop=z1))
                first[1] = False
                sfirst[1] = False
                return mpv0, mpv1

            emit_st(0)
            for idx in range(nits):
                kc, kt = its[idx]
                if idx + 1 < nits:
                    emit_st(idx + 1)
                mpv0, mpv1 = emit_sumpv(idx)
                fin0 = (kc == 2 * i and kt == KPC - 1)
                fin1 = (kc == 2 * i + 1 and kt == KPC - 1)
                if fin0:
                    finish_pre(0, mpv0)
                    finish_post(0, hh, 2 * i)
                if fin1:
                    finish_pre(1, mpv1)
                    finish_post(1, hh, 2 * i + 1)
            q_gate[qslot] = cur("s_pe")
        if i == 3:
            w("gp", "s_ob0", cur("s_ob0"))
            w("gp", "s_ob1", cur("s_ob1"))
            inc("s_cc")

            def _a2a0(E):
                with allow_ici_all_to_all():
                    return E.collective_compute(
                        "AllToAll", ALU.bypass, replica_groups=[list(range(NCORES))],
                        ins=[a2a_in0.ap()], outs=[a2a_out0.ap()])
            ops["gp"].append(lambda E: _a2a0(E).then_inc(SEM["s_cc"]))

    w("gp", "s_ob0", cur("s_ob0"))
    w("gp", "s_ob1", cur("s_ob1"))
    inc("s_cc")

    def _a2a1(E):
        with allow_ici_all_to_all():
            return E.collective_compute(
                "AllToAll", ALU.bypass, replica_groups=[list(range(NCORES))],
                ins=[a2a_in1.ap()], outs=[a2a_out1.ap()])
    ops["gp"].append(lambda E: _a2a1(E).then_inc(SEM["s_cc"]))

    fence(("pe", "act", "dve", "sync"))
    flush()
    p2.close()
    pkv.close()
    o_gat_q = [a2a_out0.ap().rearrange("(c p) t -> p c t", p=128),
               a2a_out1.ap().rearrange("(c p) t -> p c t", p=128)]

    x1_r = x1_d.rearrange("(c p) t -> p c t", p=128)
    dn_r = dn_d.ap().rearrange("(c p) t -> p c t", p=128)
    out_r = out_fm.rearrange("(c p) t -> p c t", p=128)

    for qc in range(NCHK):
        cols = slice(qc * CH, (qc + 1) * CH)
        pq = ExitStack()
        la_g = pq.enter_context(nc.sbuf_tensor(f"la_g_{qc}", [128, CH], BF16))
        la_u = pq.enter_context(nc.sbuf_tensor(f"la_u_{qc}", [128, CH], BF16))
        lad_b = pq.enter_context(nc.sbuf_tensor(f"lad_b_{qc}", [128, CH], BF16))
        vz1 = dve(lambda E: E.memset(la_g[:, :], 0))
        vz2 = dve(lambda E: E.memset(la_u[:, :], 0))
        vz3 = dve(lambda E: E.memset(lad_b[:, :], 0))
        ln3b = pq.enter_context(nc.sbuf_tensor(f"ln3b_{qc}", [128, NDC, CH], BF16))

        p3 = ExitStack()
        o_sb = p3.enter_context(nc.sbuf_tensor(f"o_sb_{qc}", [128, NDC, CH], BF16))
        wo_sb = [p3.enter_context(nc.sbuf_tensor(f"wo_sb{i}_{qc}", [128, NDC, 128], BF16)) for i in range(2)]
        ap_sb = p3.enter_context(nc.sbuf_tensor(f"ap_sb_{qc}", [128, NDC, CH], BF16))
        sqb_t = p3.enter_context(nc.sbuf_tensor(f"sqb_t_{qc}", [128, 2, CH], BF16))
        ab_t = p3.enter_context(nc.sbuf_tensor(f"ab_t_{qc}", [128, CH], BF16))
        x_t = [p3.enter_context(nc.sbuf_tensor(f"x_t{i}_{qc}", [128, CH], F32)) for i in range(2)]
        x1_t = [p3.enter_context(nc.sbuf_tensor(f"x1w_t{i}_{qc}", [128, CH], F32)) for i in range(2)]
        ap_t = p3.enter_context(nc.sbuf_tensor(f"ap_t_{qc}", [128, CH], F32))

        w("sync", "s_cc", 2 + qc)
        vo = sy_load(o_sb[:, :, :], o_gat_q[qc][:, :, :], "s_x")
        w("pe", "s_x", vo)

        wo_gate = [0, 0]
        last_stat = 0
        pend = []

        def emit_stat3(dm, va1, vsq):
            nonlocal last_stat
            w("pe", "s_act", va1)
            w("pe", "s_dve", vsq)
            pe(lambda E, dm=dm: E.matmul(pC[0:1, :], ones_b[:, :], ap_sb[:, dm, :],
                                         start=(dm == 0), stop=(dm == NDC - 1)), tick=False)
            last_stat = pe(lambda E, dm=dm: E.matmul(pD[0:1, :], ones_b[:, :], sqb_t[:, dm % 2, :],
                                                     start=(dm == 0), stop=(dm == NDC - 1)))

        for dm in range(NDC):
            slot = dm % 2
            sname = "s_woa" if slot == 0 else "s_wob"
            w("gp", "s_pe", wo_gate[slot])
            vw = gp_load(wo_sb[slot][:, :, :], wo_t[dm], sname)
            w("pe", sname, vw)
            bank = pA if slot == 0 else pB
            for f in range(NDC):
                mq = pe(lambda E, f=f, s=slot, bk=bank: E.matmul(
                    bk[:, :], wo_sb[s][:, f, :], o_sb[:, f, :],
                    start=(f == 0), stop=(f == NDC - 1)), tick=(f == NDC - 1))
            wo_gate[slot] = mq
            w("act", "s_pe", mq)
            va1 = act(lambda E, dm=dm, bk=bank: E.copy(ap_sb[:, dm, :], bk[:, :]))
            w("pe", "s_act", va1)
            w("dve", "s_act", va1)
            if last_stat:
                w("dve", "s_pe", last_stat)
            vsq = dve(lambda E, dm=dm: E.tensor_mul(sqb_t[:, dm % 2, :], ap_sb[:, dm, :], ap_sb[:, dm, :]))
            pend.append((dm, va1, vsq))
            if len(pend) > 1:
                emit_stat3(*pend.pop(0))
        while pend:
            emit_stat3(*pend.pop(0))

        bdone, stat_free = ln_rows(pC, pD, last_stat)
        w("dve", "s_pe", bdone)
        xt_gate = [0, 0]
        x1s_mark = [0] * NDC
        va2 = 0
        ln3_gate = 0
        for dm in range(NDC):
            slot = dm % 2
            sname = "s_xta" if slot == 0 else "s_xtb"
            ssem = "s_1sa" if slot == 0 else "s_1sb"
            w("sync", "s_dve", xt_gate[slot])
            vx = sy_load(x_t[slot][:, :], x_r[:, dm, cols], sname)
            if dm > 1:
                w("dve", ssem, x1s_mark[dm - 2])
            if dm > 0:
                w("dve", "s_dve", va2)
            t1 = dve(lambda E, dm=dm: E.scalar_tensor_tensor(
                ap_t[:, :], ap_sb[:, dm, :], 1.0, mean_t, ALU.mult, ALU.subtract))
            w("dve", "s_dve", t1)
            t2 = dve(lambda E: E.tensor_tensor(ap_t[:, :], ap_t[:, :], rstd_t, ALU.mult))
            w("dve", sname, vx)
            w("dve", "s_dve", t2)
            va2 = dve(lambda E, dm=dm, s=slot: E.tensor_tensor(
                x1_t[s][:, :], ap_t[:, :], x_t[s][:, :], ALU.add))
            xt_gate[slot] = va2
            w("dve", "s_dve", va2)
            w("dve", "s_pe", ln3_gate)
            vc1 = dve(lambda E, s=slot: E.tensor_copy(ab_t[:, :], x1_t[s][:, :]))
            w("dve", "s_dve", vc1)
            vc2 = dve(lambda E: E.tensor_mul(sqb_t[:, 0, :], ab_t[:, :], ab_t[:, :]))
            w("pe", "s_dve", vc2)
            if dm == 0:
                w("pe", "s_act", stat_free)
            pe(lambda E, dm=dm: E.matmul(pC[0:1, :], ones_b[:, :], ab_t[:, :],
                                         start=(dm == 0), stop=(dm == NDC - 1)), tick=False)
            last_stat = pe(lambda E, dm=dm: E.matmul(pD[0:1, :], ones_b[:, :], sqb_t[:, 0, :],
                                                     start=(dm == 0), stop=(dm == NDC - 1)))
            ln3_gate = last_stat
            w("gp", "s_dve", va2)
            x1s_mark[dm] = gp_store(x1_r[:, dm, cols], x1_t[slot][:, :], ssem)

        bdone, stat_free = ln_rows(pC, pD, last_stat, stA_gate=va2)
        w("dve", "s_pe", bdone)
        x1l_gate = list(xt_gate)
        vb3 = 0
        for dm in range(NDC):
            slot = dm % 2
            lsem = "s_x1ta" if slot == 0 else "s_x1tb"
            ssem = "s_1sa" if slot == 0 else "s_1sb"
            w("sync", ssem, x1s_mark[dm])
            w("sync", "s_dve", x1l_gate[slot])
            vxl = sy_load(x_t[slot][:, :], x1_r[:, dm, cols], lsem)
            w("dve", lsem, vxl)
            t1 = dve(lambda E, dm=dm, s=slot: E.scalar_tensor_tensor(
                ln3b[:, dm, :], x_t[s][:, :], 1.0, mean_t, ALU.mult, ALU.subtract))
            w("dve", "s_dve", t1)
            vb3 = dve(lambda E, dm=dm: E.tensor_tensor(ln3b[:, dm, :], ln3b[:, dm, :], rstd_t, ALU.mult))
            x1l_gate[slot] = vb3
        w("pe", "s_dve", vb3)

        mla = 0
        for dc in range(NDC):
            if dc == 0:
                w("pe", "s_act", stat_free)
            pe(lambda E, dc=dc: E.matmul(pC[0:RANK, :], ga_sb[:, dc, :], ln3b[:, dc, :],
                                         start=(dc == 0), stop=(dc == NDC - 1)), tick=False)
            mla = pe(lambda E, dc=dc: E.matmul(pD[0:RANK, :], ua_sb[:, dc, :], ln3b[:, dc, :],
                                               start=(dc == 0), stop=(dc == NDC - 1)))
        w("act", "s_pe", mla)
        w("act", "s_dve", vz2)
        act(lambda E: E.copy(la_g[0:RANK, :], pC[0:RANK, :]))
        vlu = act(lambda E: E.copy(la_u[0:RANK, :], pD[0:RANK, :]))
        w("pe", "s_act", vlu)

        fence()
        flush()
        p3.close()

        ph = ExitStack()
        h_sb = ph.enter_context(nc.sbuf_tensor(f"h_sb_{qc}", [128, NHC, CH], BF16))
        p4a = ExitStack()
        gw_sb = [p4a.enter_context(nc.sbuf_tensor(f"gw_sb{i}_{qc}", [128, NDC + 1, 128], BF16)) for i in range(2)]
        uw_sb = [p4a.enter_context(nc.sbuf_tensor(f"uw_sb{i}_{qc}", [128, NDC + 1, 128], BF16)) for i in range(2)]
        s_t = p4a.enter_context(nc.sbuf_tensor(f"s_t_{qc}", [128, CH], BF16))
        u_t = p4a.enter_context(nc.sbuf_tensor(f"u_t_{qc}", [128, CH], BF16))

        gu_gate = [0, 0]
        bank_gate = [0, 0]
        sil_gate = 0
        vh = 0
        mld = 0
        lad_pend = []
        for hc in range(NHC):
            slot = hc % 2
            sname = "s_gua" if slot == 0 else "s_gub"
            w("gp", "s_pe", gu_gate[slot])
            gp_load(gw_sb[slot][:, :, :], gw_t[hc], sname)
            vwu = gp_load(uw_sb[slot][:, :, :], uw_t[hc], sname)
            w("pe", sname, vwu)
            bank = stX[slot]
            w("pe", "s_act", bank_gate[slot])
            for dc in range(NDC):
                pe(lambda E, dc=dc, s=slot, bk=bank: E.matmul(
                    bk[:, 0:CH], gw_sb[s][:, dc, :], ln3b[:, dc, :],
                    start=(dc == 0), stop=False), tick=False)
            pe(lambda E, s=slot, bk=bank: E.matmul(
                bk[:, 0:CH], gw_sb[s][:, NDC, :], la_g[:, :],
                start=False, stop=True), tick=False)
            for dc in range(NDC):
                pe(lambda E, dc=dc, s=slot, bk=bank: E.matmul(
                    bk[:, CH:2 * CH], uw_sb[s][:, dc, :], ln3b[:, dc, :],
                    start=(dc == 0), stop=False), tick=False)
            mu = pe(lambda E, s=slot, bk=bank: E.matmul(
                bk[:, CH:2 * CH], uw_sb[s][:, NDC, :], la_u[:, :],
                start=False, stop=True))
            gu_gate[slot] = mu
            w("act", "s_pe", mu)
            w("act", "s_dve", sil_gate)
            act(lambda E, bk=bank: E.activation(s_t[:, :], bk[:, 0:CH], AF.Silu))
            vu_ = act(lambda E, bk=bank: E.copy(u_t[:, :], bk[:, CH:2 * CH]))
            bank_gate[slot] = vu_
            w("dve", "s_act", vu_)
            vh = dve(lambda E, hc=hc: E.tensor_tensor(h_sb[:, hc, :], s_t[:, :], u_t[:, :], ALU.mult))
            sil_gate = vh
            lad_pend.append((hc, vh))
            if len(lad_pend) > 1:
                hc2, vh2 = lad_pend.pop(0)
                w("pe", "s_dve", vh2)
                if hc2 == 0:
                    w("pe", "s_act", vlu)
                mld = pe(lambda E, hc=hc2: E.matmul(pA[0:RANK, :], da_sb[:, hc, :], h_sb[:, hc, :],
                                                    start=(hc == 0), stop=False))
        while lad_pend:
            hc2, vh2 = lad_pend.pop(0)
            w("pe", "s_dve", vh2)
            if hc2 == 0:
                w("pe", "s_act", vlu)
            mld = pe(lambda E, hc=hc2: E.matmul(pA[0:RANK, :], da_sb[:, hc, :], h_sb[:, hc, :],
                                                start=(hc == 0), stop=(hc == NHC - 1)))
        w("act", "s_pe", mld)
        w("act", "s_dve", vz3)
        vle = act(lambda E: E.copy(lad_b[0:RANK, :], pA[0:RANK, :]))
        w("pe", "s_act", vle)

        fence()
        flush()
        p4a.close()

        p4b = ExitStack()
        dwh = [p4b.enter_context(nc.sbuf_tensor(f"dwh{i}_{qc}", [128, 44, 128], BF16)) for i in range(2)]
        dn_t2 = [p4b.enter_context(nc.sbuf_tensor(f"dn_t2{i}_{qc}", [128, CH], BF16)) for i in range(2)]
        sqd_t = p4b.enter_context(nc.sbuf_tensor(f"sqd_t_{qc}", [128, 2, CH], BF16))
        x1_t4 = [p4b.enter_context(nc.sbuf_tensor(f"x1r_t{i}_{qc}", [128, CH], F32)) for i in range(2)]
        out_t = [p4b.enter_context(nc.sbuf_tensor(f"out_t{i}_{qc}", [128, CH], F32)) for i in range(2)]
        dnl_t = [p4b.enter_context(nc.sbuf_tensor(f"dnl_t{i}_{qc}", [128, CH], BF16)) for i in range(2)]
        dnf_t = p4b.enter_context(nc.sbuf_tensor(f"dnf_t_{qc}", [128, CH], F32))

        dw_gate = [0, 0]
        last_stat = 0
        dns_mark = [0] * NDC
        dnt_gate = [0, 0]
        dn_pend = []

        def emit_dstat(dm, va1, vsq):
            nonlocal last_stat
            w("pe", "s_act", va1)
            w("pe", "s_dve", vsq)
            pe(lambda E, dm=dm: E.matmul(pD[0:1, :], ones_b[:, :], dn_t2[dm % 2][:, :],
                                         start=(dm == 0), stop=(dm == NDC - 1)), tick=False)
            last_stat = pe(lambda E, dm=dm: E.matmul(pA[0:1, :], ones_b[:, :], sqd_t[:, dm % 2, :],
                                                     start=(dm == 0), stop=(dm == NDC - 1)))

        HALF0, HALF1 = 44, 43
        for dm in range(NDC):
            bank = pB if dm % 2 == 0 else pC
            mdn = 0
            for half in range(2):
                sname = "s_dwa" if half == 0 else "s_dwb"
                nch = HALF0 if half == 0 else HALF1
                base = 0 if half == 0 else HALF0
                w("gp", "s_pe", dw_gate[half])
                vw = gp_load(dwh[half][:, 0:nch, :], dw_t[dm][:, base:base + nch, :], sname)
                w("pe", sname, vw)
                for lc in range(nch):
                    hc = base + lc
                    rhs_h = (lambda E, hc=hc: h_sb[:, hc, :]) if hc < NHC else (lambda E: lad_b[:, :])
                    mdn = pe(lambda E, lc=lc, half=half, bk=bank, rh=rhs_h, last=(hc == NHC): E.matmul(
                        bk[:, :], dwh[half][:, lc, :], rh(E),
                        start=(half == 0 and lc == 0), stop=last), tick=(lc == nch - 1))
                dw_gate[half] = mdn
            slot = dm % 2
            dsem = "s_dsa" if slot == 0 else "s_dsb"
            w("act", "s_pe", mdn)
            if dm > 1:
                w("act", dsem, dns_mark[dm - 2])
            w("act", "s_dve", dnt_gate[slot])
            va1 = act(lambda E, bk=bank, s=slot: E.copy(dn_t2[s][:, :], bk[:, :]))
            w("pe", "s_act", va1)
            w("dve", "s_act", va1)
            if last_stat:
                w("dve", "s_pe", last_stat)
            vsq = dve(lambda E, s=slot: E.tensor_mul(sqd_t[:, s, :], dn_t2[s][:, :], dn_t2[s][:, :]))
            dnt_gate[slot] = vsq
            dn_pend.append((dm, va1, vsq))
            if len(dn_pend) > 1:
                emit_dstat(*dn_pend.pop(0))
            w("gp", "s_act", va1)
            dns_mark[dm] = gp_store(dn_r[:, dm, cols], dn_t2[slot][:, :], dsem)
        while dn_pend:
            emit_dstat(*dn_pend.pop(0))

        bdone, stat_free = ln_rows(pD, pA, last_stat)
        w("dve", "s_pe", bdone)
        x1t_gate = [0, 0]
        ot_gate = [0, 0]
        dnl_gate = [0, 0]
        vo3 = 0
        for dm in range(NDC):
            slot = dm % 2
            lsem = "s_x1ta" if slot == 0 else "s_x1tb"
            dlsem = "s_dla" if slot == 0 else "s_dlb"
            dsem = "s_dsa" if slot == 0 else "s_dsb"
            w("sync", "s_dve", x1t_gate[slot])
            vx1 = sy_load(x1_t4[slot][:, :], x1_r[:, dm, cols], lsem)
            w("sync", dsem, dns_mark[dm])
            w("sync", "s_dve", dnl_gate[slot])
            vdl = sy_load(dnl_t[slot][:, :], dn_r[:, dm, cols], dlsem)
            if dm > 0:
                w("dve", "s_dve", vo3)
            w("dve", dlsem, vdl)
            t1 = dve(lambda E, s=slot: E.scalar_tensor_tensor(
                dnf_t[:, :], dnl_t[s][:, :], 1.0, mean_t, ALU.mult, ALU.subtract))
            dnl_gate[slot] = t1
            w("dve", "s_dve", t1)
            t2 = dve(lambda E: E.tensor_tensor(dnf_t[:, :], dnf_t[:, :], rstd_t, ALU.mult))
            w("dve", lsem, vx1)
            w("dve", "s_dve", t2)
            w("dve", "s_ot0" if slot == 0 else "s_ot1", ot_gate[slot])
            vo3 = dve(lambda E, s=slot: E.tensor_tensor(
                out_t[s][:, :], dnf_t[:, :], x1_t4[s][:, :], ALU.add))
            x1t_gate[slot] = vo3
            w("sync", "s_dve", vo3)
            ot_gate[slot] = sy_store(out_r[:, dm, cols], out_t[slot][:, :], "s_ot0" if slot == 0 else "s_ot1")
        w("pe", "s_dve", vo3)

        fence()
        flush()
        p4b.close()
        ph.close()
        pq.close()

    w("sync", "s_ot0", cur("s_ot0"))
    w("sync", "s_ot1", cur("s_ot1"))
    w("gp", "s_gst", cur("s_gst"))
    flush()
    st.close()
    return nc


def _stage(inputs):
    S = int(np.asarray(inputs["L0_attn_hidden_states"]).shape[1])
    CH = S // 16
    TPC = S // 8
    KPC = CH // 128
    hid = np.ascontiguousarray(np.asarray(inputs["L0_attn_hidden_states"], dtype=np.float32)[0])
    wq = np.asarray(inputs["L0_attn_w_q"], dtype=np.float32)
    wk = np.asarray(inputs["L0_attn_w_k"], dtype=np.float32)
    wv = np.asarray(inputs["L0_attn_w_v"], dtype=np.float32)
    wo = np.asarray(inputs["L0_attn_w_o"], dtype=np.float32)
    gw = np.asarray(inputs["L0_ffn_gate_w"], dtype=np.float32)
    uw = np.asarray(inputs["L0_ffn_up_w"], dtype=np.float32)
    dw = np.asarray(inputs["L0_ffn_down_w"], dtype=np.float32)
    ga = np.asarray(inputs["L0_ffn_gate_a"], dtype=np.float32)
    gb = np.asarray(inputs["L0_ffn_gate_b"], dtype=np.float32)
    ua = np.asarray(inputs["L0_ffn_up_a"], dtype=np.float32)
    ub = np.asarray(inputs["L0_ffn_up_b"], dtype=np.float32)
    da = np.asarray(inputs["L0_ffn_down_a"], dtype=np.float32)
    db = np.asarray(inputs["L0_ffn_down_b"], dtype=np.float32)

    def pmaj(strips):
        n, K, _ = strips.shape
        return np.ascontiguousarray(strips.reshape(n, K // 128, 128, 128).transpose(0, 2, 1, 3))

    def pad_strips(wT, b, nstr):
        K = wT.shape[0]
        out = np.zeros((nstr, K + 128, 128), np.float32)
        for i in range(nstr):
            out[i, :K, :] = wT[:, i * 128:(i + 1) * 128]
            out[i, K:K + RANK, :] = b[i * 128:(i + 1) * 128, :].T
        return pmaj(out)

    shared = {
        "wo_t": pmaj(np.ascontiguousarray(wo.reshape(DM, NDC, 128).transpose(1, 0, 2))),
        "gw_t": pad_strips(gw.T, gb, NHC),
        "uw_t": pad_strips(uw.T, ub, NHC),
        "dw_t": pad_strips(dw.T, db, NDC),
        "ga_t": np.ascontiguousarray(ga.T),
        "ua_t": np.ascontiguousarray(ua.T),
        "da_t": np.ascontiguousarray(da.T),
        "ones_i": np.ones((128, 1), np.float32),
        "onesr_i": np.ones((1, 128), np.float32),
    }
    k_idx = np.arange(128)[:, None]
    q_idx = np.arange(CH)[None, :]
    masks = np.empty((128, KPC, CH), np.float32)
    for kt in range(KPC):
        masks[:, kt, :] = ((kt * 128 + k_idx) <= q_idx).astype(np.float32)
    shared["masks_i"] = masks

    in_maps = []
    for c in range(NCORES):
        toks = np.concatenate([np.arange(c * CH, (c + 1) * CH),
                               np.arange((15 - c) * CH, (16 - c) * CH)])
        m = dict(shared)
        m["x_fm"] = np.ascontiguousarray(hid[toks].T)
        m["wq_t"] = pmaj(np.ascontiguousarray(
            wq[:, 512 * c:512 * (c + 1)].reshape(DM, 4, 128).transpose(1, 0, 2)))
        m["wk_t"] = pmaj(wk[:, 128 * c:128 * (c + 1)][None])[0]
        m["wv_t"] = pmaj(wv[:, 128 * c:128 * (c + 1)][None])[0]
        in_maps.append(m)
    return S, CH, in_maps


def kernel(**inputs):
    S, CH, in_maps = _stage(inputs)
    nc = build(S)
    res = run_bass_kernel_spmd(nc, in_maps, core_ids=list(range(NCORES)))
    out = np.empty((1, S, DM), np.float32)
    for c in range(NCORES):
        of = res.results[c]["out_fm"]
        out[0, c * CH:(c + 1) * CH] = of[:, :CH].T
        out[0, (15 - c) * CH:(16 - c) * CH] = of[:, CH:].T
    return out
